# version 13
# baseline (speedup 1.0000x reference)
import os
import contextlib
import numpy as np
import concourse.bass as bass
import concourse.mybir as mybir
from concourse.bass_utils import run_bass_kernel_spmd

F32 = mybir.dt.float32
BF16 = mybir.dt.bfloat16
AF = mybir.ActivationFunctionType
ALU = mybir.AluOpType

B, S, E, H, D, CH, HID = 4, 4096, 2048, 16, 128, 256, 8192
TOK = 2048
NT = 4
NTP = 8
NCH = 8
ROPE_BASE = 10000.0
PERM = np.concatenate([np.arange(0, D, 2), np.arange(1, D, 2)])
NCORES = 8

DEBUG = os.environ.get("ATHENA_KERNEL_DEBUG", "0") == "1"
PHASES = int(os.environ.get("ATHENA_KERNEL_PHASES", "6"))
A_GRP = int(os.environ.get("ATHENA_A_GRP", "2"))
A_HEADS = int(os.environ.get("ATHENA_A_HEADS", "8"))
A_CHUNKS = int(os.environ.get("ATHENA_A_CHUNKS", "8"))
NO_ILV = os.environ.get("ATHENA_NO_ILV", "0") == "1"
ILV_AT = int(os.environ.get("ATHENA_ILV_AT", "1"))
ILV_K = int(os.environ.get("ATHENA_ILV_K", "0"))



def _rope_tables(pos0):
    invf = 1.0 / ROPE_BASE ** (np.arange(0, D, 2, dtype=np.float64) / D)
    pos = np.arange(pos0, pos0 + TOK, dtype=np.float64)
    freqs = np.outer(pos, invf)
    emb = np.concatenate([freqs, freqs], -1)
    cos = np.ascontiguousarray(np.cos(emb).astype(np.float32)[:, PERM].T)
    sin = np.sin(emb).astype(np.float32)[:, PERM].T.copy()
    sin[:64, :] *= -1.0
    return cos, np.ascontiguousarray(sin)


def _stage(inputs):
    x = np.asarray(inputs["L0_attn_x"], dtype=np.float32)
    Wqkv = np.asarray(inputs["L0_attn_W_qkv"], dtype=np.float32)
    Wout = np.asarray(inputs["L0_attn_W_out"], dtype=np.float32)
    Wg = np.asarray(inputs["L0_ffn_w_gate"], dtype=np.float32)
    Wu = np.asarray(inputs["L0_ffn_w_up"], dtype=np.float32)
    Wd = np.asarray(inputs["L0_ffn_w_down"], dtype=np.float32)
    Wq, Wk, Wv = Wqkv[0:E], Wqkv[E:2 * E], Wqkv[2 * E:3 * E]

    def lhsT_headed(W, nh, perm=None):
        A = W.reshape(nh, 128, 16, 128).transpose(0, 3, 2, 1)
        if perm is not None:
            A = A[:, :, :, perm]
        return np.ascontiguousarray(A).reshape(nh, 128, 16 * 128)

    wq = lhsT_headed(Wq, H, PERM)
    wk = lhsT_headed(Wk, H, PERM)
    wv = np.ascontiguousarray(
        Wv.reshape(4, 512, 16, 128).transpose(0, 3, 2, 1)).reshape(4, 128, 8192)
    wg = Wg.reshape(64, 128, 16, 128).transpose(0, 3, 2, 1).reshape(64, 128, 2048)
    wu = Wu.reshape(64, 128, 16, 128).transpose(0, 3, 2, 1).reshape(64, 128, 2048)
    wgu = np.ascontiguousarray(np.concatenate([wg, wu], axis=2))
    wd = Wd.reshape(16, 128, 64, 128).transpose(0, 3, 2, 1).reshape(16, 128, 8192)
    wo = Wout.reshape(16, 128, 16, 128).transpose(0, 3, 2, 1).reshape(16, 128, 2048)
    wdo = np.ascontiguousarray(np.concatenate([wd, wo], axis=2))

    x_fm = np.ascontiguousarray(x.transpose(0, 2, 1))

    def stage_x(b, t0, ntile, tl):
        sl = x_fm[b][:, t0:t0 + ntile * tl]
        A = sl.reshape(16, 128, ntile, tl).transpose(2, 1, 0, 3)
        return np.ascontiguousarray(A).reshape(ntile, 128, 16 * tl)

    cos0, sin0 = _rope_tables(0)
    cos1, sin1 = _rope_tables(TOK)
    ones2d = np.ones((128, 128), np.float32)
    tri = (np.arange(128)[None, :] >= np.arange(128)[:, None]).astype(np.float32)
    ident = np.eye(128, dtype=np.float32)
    zpre = np.zeros((NTP, 128, 16 * 256), np.float32)

    shared = dict(wq=wq, wk=wk, wv=wv, wgu=wgu, wdo=wdo, ones2d=ones2d,
                  tri=tri, ident=ident, cos_pre=cos0, sins_pre=sin0)
    in_maps = []
    for c in range(NCORES):
        b, half = c // 2, c % 2
        m = dict(shared)
        m["x_own"] = stage_x(b, half * TOK, NT, 512)
        m["x_pre"] = stage_x(b, 0, NTP, 256) if half == 1 else zpre
        m["cos_own"] = cos1 if half == 1 else cos0
        m["sins_own"] = sin1 if half == 1 else sin0
        m["aug"] = np.full((128, 1), float(half), np.float32)
        in_maps.append(m)
    return in_maps



class _Cut(Exception):
    pass


class Sem:
    def __init__(self, h):
        self.h = h
        self.n = 0

    def inc(self, k=1):
        self.n += k
        return self.n


class Bld:
    ENGS = ("sync", "tensor", "vector", "scalar", "gpsimd")

    def __init__(self, nc, block, es):
        self.nc = nc
        self.block = block
        self.q = {e: [] for e in self.ENGS}
        self.sem = {n: Sem(es.enter_context(nc.semaphore(n)))
                    for n in ("dW", "dX", "dY", "pe", "dve", "act", "pool", "bar")}
        self.rope_prev_exp = 0
        self.t_wpp = {}
        self.t_wppc = {}

    def wait(self, eng, sem, val):
        if not val or val <= 0:
            return
        h = self.sem[sem].h
        self.q[eng].append(lambda e, h=h, v=val: e.wait_ge(h, v))

    def op(self, eng, fn, inc=None):
        if inc is None:
            self.q[eng].append(lambda e, f=fn: f(e))
            return None
        name, k = (inc, 1) if isinstance(inc, str) else inc
        s = self.sem[name]
        tgt = s.inc(k)
        self.q[eng].append(lambda e, f=fn, h=s.h, k=k: f(e).then_inc(h, k))
        return tgt

    def dma(self, eng, dst, src, sem):
        return self.op(eng, lambda e, d=dst, s=src: e.dma_start(out=d, in_=s),
                       (sem, 16))

    def flush(self):
        for eng in self.ENGS:
            fns = self.q[eng]
            if not fns:
                continue
            self.q[eng] = []
            getattr(self.block, eng)(lambda e, fns=fns: [f(e) for f in fns])

    def barrier(self):
        self.wait("sync", "dW", self.sem["dW"].n)
        self.wait("scalar", "dX", self.sem["dX"].n)
        self.wait("scalar", "dY", self.sem["dY"].n)
        bar = self.sem["bar"]
        for eng in self.ENGS:
            self.q[eng].append(
                lambda e, h=bar.h: (e.drain(), e.sem_inc(h, 1)))
        bar.n += 5
        tgt = bar.n
        for eng in self.ENGS:
            self.wait(eng, "bar", tgt)
        self.flush()



def _cp(e, out, in_):
    if hasattr(e, "tensor_copy"):
        return e.tensor_copy(out=out, in_=in_)
    return e.copy(out=out, in_=in_)

def _rope_elu(bld, p_src, cos_ap, sins_lo, sins_hi, s1, s2, s3, dst, width):
    bld.op("vector", lambda e: e.tensor_tensor(
        out=s1[:, 0:width], in0=p_src[:, 0:width], in1=cos_ap, op=ALU.mult))
    bld.op("vector", lambda e: e.tensor_tensor(
        out=s2[0:64, 0:width], in0=p_src[64:128, 0:width], in1=sins_lo,
        op=ALU.mult))
    t_psum_done = bld.op("vector", lambda e: e.tensor_tensor(
        out=s2[64:128, 0:width], in0=p_src[0:64, 0:width], in1=sins_hi,
        op=ALU.mult), inc="dve")
    bld.wait("vector", "act", bld.rope_prev_exp)
    bld.op("vector", lambda e: e.tensor_tensor(
        out=s3[:, 0:width], in0=s1[:, 0:width], in1=s2[:, 0:width], op=ALU.add))
    t_pool = bld.op("vector", lambda e: e.tensor_scalar_min(
        out=s1[:, 0:width], in0=s3[:, 0:width], scalar1=0.0), inc="dve")
    bld.wait("scalar", "dve", t_pool)
    t_exp = bld.op("scalar", lambda e: e.activation(
        out=s2[:, 0:width], in_=s1[:, 0:width], func=AF.Exp), inc="act")
    bld.rope_prev_exp = t_exp
    bld.wait("vector", "act", t_exp)
    t_done = bld.op("vector", lambda e: e.scalar_tensor_tensor(
        out=dst, in0=s3[:, 0:width], scalar=0.0, in1=s2[:, 0:width],
        op0=ALU.max, op1=ALU.add), inc="dve")
    return t_psum_done, t_done



def build_nc(debug=False, phases=None):
    if phases is None:
        phases = PHASES
    nc = bass.Bass("TRN2", target_bir_lowering=False)

    def din(name, shape, dt=F32):
        return nc.dram_tensor(name, shape, dt, kind="ExternalInput").ap()

    x_own = din("x_own", [NT, 128, 16 * 512])
    x_pre = din("x_pre", [NTP, 128, 16 * 256])
    wq_d = din("wq", [H, 128, E])
    wk_d = din("wk", [H, 128, E])
    wv_d = din("wv", [4, 128, 8192])
    wgu_d = din("wgu", [64, 128, 4096])
    wdo_d = din("wdo", [16, 128, 10240])
    ones_d = din("ones2d", [128, 128])
    tri_d = din("tri", [128, 128])
    ident_d = din("ident", [128, 128])
    cos_own_d = din("cos_own", [128, TOK])
    sins_own_d = din("sins_own", [128, TOK])
    cos_pre_d = din("cos_pre", [128, TOK])
    sins_pre_d = din("sins_pre", [128, TOK])
    aug_d = din("aug", [128, 1])

    y_out = nc.dram_tensor("y", [16, 128, TOK], F32, kind="ExternalOutput").ap()
    sp_attn = nc.dram_tensor("sp_attn", [NT, H, 128, 512], BF16, kind="Internal").ap()
    sp_gated = nc.dram_tensor("sp_gated", [NT, 64, 128, 512], BF16, kind="Internal").ap()
    wdo_bf_d = nc.dram_tensor("wdo_bf", [16, 128, 10240], BF16, kind="Internal").ap()

    dbg = {}
    if debug:
        for nm, shp, dt in (("xn", [128, 16 * TOK], BF16), ("S", [128, H * 129], F32),
                            ("q0", [128, TOK], BF16), ("k0", [128, TOK], BF16),
                            ("att0", [128, TOK], BF16), ("v0", [128, 2048], BF16),
                            ("gated0", [128, TOK], BF16), ("wdo0", [128, 10240], BF16),
                            ("attnall", [128, H * TOK], BF16),
                            ("gt0", [128, 64 * 512], BF16), ("at0", [128, 16 * 512], BF16)):
            dbg[nm] = nc.dram_tensor("dbg_" + nm, shp, dt, kind="ExternalOutput").ap()

    es = contextlib.ExitStack()
    block = es.enter_context(nc.Block())
    SS = es.enter_context(contextlib.ExitStack())
    ones_s = SS.enter_context(nc.sbuf_tensor("ones_s", [128, 128], F32))
    ident_b = SS.enter_context(nc.sbuf_tensor("ident_b", [128, 128], BF16))
    onescol_b = SS.enter_context(nc.sbuf_tensor("onescol_b", [128, 1], BF16))
    aug_b = SS.enter_context(nc.sbuf_tensor("aug_b", [128, 1], BF16))
    S_all = SS.enter_context(nc.sbuf_tensor("S_all", [128, H * 129], F32))
    bld = Bld(nc, block, es)

    try:
        with nc.sbuf_tensor("cst", [128, 128], F32) as cst, \
             nc.sbuf_tensor("cst2", [128, 1], F32) as cst2:
            bld.dma("sync", ones_s[:], ones_d, "dW")
            bld.dma("sync", cst[:], ident_d, "dW")
            t_l = bld.dma("sync", cst2[:], aug_d, "dW")
            bld.wait("vector", "dW", t_l)
            bld.op("vector", lambda e: e.tensor_copy(out=ident_b[:], in_=cst[:]))
            bld.op("vector", lambda e: e.tensor_copy(out=onescol_b[:], in_=ones_s[:, 0:1]))
            bld.op("vector", lambda e: e.tensor_copy(out=aug_b[:], in_=cst2[:]))
            bld.op("vector", lambda e: e.memset(S_all[:], 0.0), inc="dve")
            bld.barrier()
        if phases <= 1:
            raise _Cut

        def ln_tile(x_s, xn_dst16, TL, sq_s, s_sub, p_st, p_st2, p_mb, p_rb,
                    t_xload, t_prev_done, rows=None):
            row_m, row_b, row_r, row_c = rows
            """Returns dve target when all xn writes (and all x_s reads) done."""
            bld.wait("tensor", "dW", t_xload)
            bld.wait("tensor", "dve", t_prev_done)
            for fc in range(16):
                bld.op("tensor", lambda e, fc=fc: e.matmul(
                    out=p_st[:], lhsT=ones_s[:, 0:1], rhs=x_s[:, fc * TL:(fc + 1) * TL],
                    start=(fc == 0), stop=(fc == 15)))
            bld.wait("scalar", "dW", t_xload)
            bld.wait("scalar", "dve", t_prev_done)
            t_mm = [0] * 16
            for fc in range(16):
                if fc >= 2:
                    bld.wait("scalar", "pe", t_mm[fc - 2])
                t_sq = bld.op("scalar", lambda e, fc=fc: e.activation(
                    out=sq_s[:, (fc % 2) * TL:(fc % 2 + 1) * TL],
                    in_=x_s[:, fc * TL:(fc + 1) * TL], func=AF.Square), inc="act")
                bld.wait("tensor", "act", t_sq)
                t_mm[fc] = bld.op("tensor", lambda e, fc=fc: e.matmul(
                    out=p_st2[:], lhsT=ones_s[:, 0:1],
                    rhs=sq_s[:, (fc % 2) * TL:(fc % 2 + 1) * TL],
                    start=(fc == 0), stop=(fc == 15)), inc="pe")
            t_stats = t_mm[15]
            bld.wait("vector", "pe", t_stats)
            bld.op("vector", lambda e: e.tensor_scalar_mul(
                out=row_m[:, 0:TL], in0=p_st[:], scalar1=1.0 / E))
            bld.op("vector", lambda e: e.tensor_tensor(
                out=row_b[:, 0:TL], in0=row_m[:, 0:TL], in1=row_m[:, 0:TL], op=ALU.mult))
            bld.op("vector", lambda e: e.scalar_tensor_tensor(
                out=row_c[:, 0:TL], in0=p_st2[:], scalar=1.0 / E, in1=row_b[:, 0:TL],
                op0=ALU.mult, op1=ALU.subtract))
            t_fin = bld.op("vector", lambda e: e.tensor_scalar_add(
                out=row_b[:, 0:TL], in0=row_c[:, 0:TL], scalar1=1e-5), inc="dve")
            bld.wait("scalar", "dve", t_fin)
            bld.op("scalar", lambda e: e.activation(
                out=row_r[:, 0:TL], in_=row_b[:, 0:TL], func=AF.Ln))
            t_r = bld.op("scalar", lambda e: e.activation(
                out=row_c[:, 0:TL], in_=row_r[:, 0:TL], func=AF.Exp, scale=-0.5),
                inc="act")
            bld.wait("tensor", "dve", t_fin)
            bld.op("tensor", lambda e: e.matmul(
                out=p_mb[:], lhsT=ones_s[0:1, :], rhs=row_m[0:1, 0:TL],
                start=True, stop=True))
            bld.wait("tensor", "act", t_r)
            t_bc = bld.op("tensor", lambda e: e.matmul(
                out=p_rb[:], lhsT=ones_s[0:1, :], rhs=row_c[0:1, 0:TL],
                start=True, stop=True), inc="pe")
            bld.wait("vector", "pe", t_bc)
            t_xn = None
            for fc in range(16):
                bld.op("vector", lambda e, fc=fc: e.tensor_tensor(
                    out=s_sub[:, 0:TL], in0=x_s[:, fc * TL:(fc + 1) * TL], in1=p_mb[:],
                    op=ALU.subtract))
                t_xn = bld.op("vector", lambda e, fc=fc: e.tensor_tensor(
                    out=xn_dst16(fc), in0=s_sub[:, 0:TL], in1=p_rb[:], op=ALU.mult),
                    inc=("dve" if fc == 15 else None))
            return t_xn

        with contextlib.ExitStack() as PS:
            cos_p = [PS.enter_context(nc.sbuf_tensor("cos_p0", [128, 256], F32))] * 2
            sin_p = [PS.enter_context(nc.sbuf_tensor("sin_p0", [128, 256], F32))] * 2
            wstg = PS.enter_context(nc.sbuf_tensor("wstg", [128, 1024], F32))
            wk_b = PS.enter_context(nc.sbuf_tensor("wk_b", [128, H * 2048], BF16))
            wv_b = PS.enter_context(nc.sbuf_tensor("wv_b", [128, 4 * 8192], BF16))
            xp = [PS.enter_context(nc.sbuf_tensor(f"xp{i}", [128, 16 * 256], F32))
                  for i in range(2)]
            xnp = [PS.enter_context(nc.sbuf_tensor(f"xnp{i}", [128, 16 * 256], BF16))
                   for i in range(2)]
            kp_b = PS.enter_context(nc.sbuf_tensor("kp_b", [128, 256], BF16))
            ktm = PS.enter_context(nc.sbuf_tensor("ktm", [128, 256], BF16))
            vp_b = PS.enter_context(nc.sbuf_tensor("vp_b", [128, 2 * 2048], BF16))
            sq_s = PS.enter_context(nc.sbuf_tensor("sq_p", [128, 2 * 256], F32))
            s1 = PS.enter_context(nc.sbuf_tensor("s1p", [128, 256], F32))
            s2 = PS.enter_context(nc.sbuf_tensor("s2p", [128, 256], F32))
            s3 = PS.enter_context(nc.sbuf_tensor("s3p", [128, 256], F32))
            rows_p = tuple(PS.enter_context(nc.sbuf_tensor(f"rowp{i}", [1, 256], F32))
                           for i in range(2))
            rows_p = (rows_p[0], rows_p[1], sq_s[0:1, 256:512], sq_s[0:1, 0:256])
            p_st = PS.enter_context(nc.psum_tensor("p_st", [1, 256], F32))
            p_st2 = PS.enter_context(nc.psum_tensor("p_st2", [1, 256], F32))
            p_mb = PS.enter_context(nc.psum_tensor("p_mb", [128, 256], F32))
            p_rb = PS.enter_context(nc.psum_tensor("p_rb", [128, 256], F32))
            p_k = PS.enter_context(nc.psum_tensor("p_k", [128, 256], F32))
            p_v = PS.enter_context(nc.psum_tensor("p_v", [128, 512], F32))
            p_tr = PS.enter_context(nc.psum_tensor("p_tr", [128, 256], BF16))
            p_su = PS.enter_context(nc.psum_tensor("p_su", [128, 129], F32))

            for which, dstb in (("k", wk_b), ("v", wv_b)):
                for i in range(32):
                    if which == "k":
                        piece = wk_d[i // 2][:, (i % 2) * 1024:(i % 2 + 1) * 1024]
                    else:
                        piece = wv_d[i // 8][:, (i % 8) * 1024:(i % 8 + 1) * 1024]
                    t_l = bld.dma("sync", wstg[:], piece, "dW")
                    eng, sem = (("vector", "dve") if i % 2 == 0 else ("scalar", "act"))
                    bld.wait(eng, "dW", t_l)
                    t = bld.op(eng, lambda e, i=i, dstb=dstb: _cp(e,
                        dstb[:, i * 1024:(i + 1) * 1024], wstg[:]), inc=sem)
                    bld.wait("sync", sem, t)
            bld.barrier()

            TL = 256
            t_xn_prev = 0
            t_xn_par = {0: 0, 1: 0}
            t_kv_done = {0: 0, 1: 0}
            t_rope_par = {0: 0, 1: 0}
            t_vevac = 0
            t_vmm = 0
            t_ktm_evac = 0
            t_kread_prev = 0
            t_sadd_prev = 0
            for tp in range(NTP):
                par = tp % 2
                xpb, xnpb = xp[par], xnp[par]
                cosb, sinb = cos_p[par], sin_p[par]
                bld.wait("sync", "dve", t_xn_par[par])
                bld.wait("sync", "dve", max(t_rope_par[0], t_rope_par[1]))
                bld.dma("sync", cosb[:], cos_pre_d[:, tp * 256:(tp + 1) * 256], "dW")
                bld.dma("sync", sinb[:], sins_pre_d[:, tp * 256:(tp + 1) * 256], "dW")
                t_x = bld.dma("sync", xpb[:], x_pre[tp], "dW")
                bld.wait("vector", "pe", t_kv_done[par])
                t_xn = ln_tile(xpb, lambda fc: xnpb[:, fc * TL:(fc + 1) * TL], TL,
                               sq_s, s1, p_st, p_st2, p_mb, p_rb, t_x, t_xn_prev,
                               rows=rows_p)
                t_xn_prev = t_xn
                t_xn_par[par] = t_xn
                bld.wait("tensor", "dve", t_xn)
                for ts in range(2):
                    for hg in range(4):
                        bld.wait("tensor", "act", t_vevac)
                        for fc in range(16):
                            t_vmm = bld.op("tensor", lambda e, ts=ts, hg=hg, fc=fc, xnpb=xnpb: e.matmul(
                                out=p_v[:],
                                lhsT=xnpb[:, fc * TL + ts * 128:fc * TL + ts * 128 + 128],
                                rhs=wv_b[:, hg * 8192 + fc * 512:hg * 8192 + (fc + 1) * 512],
                                start=(fc == 0), stop=(fc == 15)),
                                inc=("pe" if fc == 15 else None))
                        bld.wait("scalar", "pe", t_vmm)
                        t_vevac = bld.op("scalar", lambda e, ts=ts, hg=hg: e.copy(
                            out=vp_b[:, ts * 2048 + hg * 512:ts * 2048 + (hg + 1) * 512],
                            in_=p_v[:]), inc="act")
                t_vall = t_vevac
                for h in range(H):
                    bld.wait("tensor", "dve", t_kread_prev)
                    for fc in range(16):
                        t_kmm = bld.op("tensor", lambda e, h=h, fc=fc, xnpb=xnpb: e.matmul(
                            out=p_k[:],
                            lhsT=wk_b[:, h * 2048 + fc * 128:h * 2048 + fc * 128 + 128],
                            rhs=xnpb[:, fc * TL:(fc + 1) * TL],
                            start=(fc == 0), stop=(fc == 15)),
                            inc=("pe" if fc == 15 else None))
                    bld.wait("vector", "pe", t_kmm)
                    bld.wait("vector", "dW", t_x)
                    t_kread_prev, t_kdone = _rope_elu(
                        bld, p_k, cosb[:, 0:TL],
                        sinb[0:64, 0:TL], sinb[64:128, 0:TL],
                        s1, s2, s3, kp_b[:, 0:TL], TL)
                    t_rope_par[par] = t_kread_prev
                    bld.wait("tensor", "dve", t_kdone)
                    bld.wait("tensor", "act", t_ktm_evac)
                    bld.op("tensor", lambda e: e.transpose(
                        out=p_tr[:, 0:128], in_=kp_b[:, 0:128], identity=ident_b[:]))
                    t_trmm = bld.op("tensor", lambda e: e.transpose(
                        out=p_tr[:, 128:256], in_=kp_b[:, 128:256], identity=ident_b[:]),
                        inc="pe")
                    bld.wait("scalar", "pe", t_trmm)
                    t_ktm_evac = bld.op("scalar", lambda e: e.copy(
                        out=ktm[:], in_=p_tr[:]), inc="act")
                    bld.wait("tensor", "act", t_ktm_evac)
                    bld.wait("tensor", "act", t_vall)
                    bld.wait("tensor", "dve", t_sadd_prev)
                    t_su = None
                    for ts in range(2):
                        voff = ts * 2048 + (h // 4) * 512 + (h % 4) * 128
                        bld.op("tensor", lambda e, ts=ts, voff=voff: e.matmul(
                            out=p_su[:, 0:128], lhsT=ktm[:, ts * 128:(ts + 1) * 128],
                            rhs=vp_b[:, voff:voff + 128],
                            start=(ts == 0), stop=False))
                        t_su = bld.op("tensor", lambda e, ts=ts: e.matmul(
                            out=p_su[:, 128:129], lhsT=ktm[:, ts * 128:(ts + 1) * 128],
                            rhs=aug_b[:], start=False, stop=(ts == 1)),
                            inc=("pe" if ts == 1 else None))
                    bld.wait("vector", "pe", t_su)
                    t_sadd_prev = bld.op("vector", lambda e, h=h: e.tensor_tensor(
                        out=S_all[:, h * 129:(h + 1) * 129],
                        in0=S_all[:, h * 129:(h + 1) * 129],
                        in1=p_su[:], op=ALU.add), inc="dve")
                bld.flush()
            bld.barrier()

        if debug:
            bld.dma("scalar", dbg["S"], S_all[:], "dY")
            bld.barrier()
        if phases <= 2:
            raise _Cut

        def _main_scope():
          with contextlib.ExitStack() as MS:
            xn = MS.enter_context(nc.sbuf_tensor("xn", [128, 16 * TOK], BF16))
            cos_o = MS.enter_context(nc.sbuf_tensor("cos_o", [128, TOK], F32))
            sin_o = MS.enter_context(nc.sbuf_tensor("sin_o", [128, TOK], F32))
            bld.dma("sync", cos_o[:], cos_own_d, "dW")
            bld.dma("sync", sin_o[:], sins_own_d, "dW")

            with contextlib.ExitStack() as P0:
                x_tt = P0.enter_context(nc.sbuf_tensor("x_tt", [128, 16 * 512], F32))
                sq0 = P0.enter_context(nc.sbuf_tensor("sq0", [128, 2 * 512], F32))
                s10 = P0.enter_context(nc.sbuf_tensor("s10", [128, 512], F32))
                rows_0 = tuple(P0.enter_context(nc.sbuf_tensor(f"row0{i}", [1, 512], F32))
                               for i in range(4))
                p_st = P0.enter_context(nc.psum_tensor("p_st0", [1, 512], F32))
                p_st2 = P0.enter_context(nc.psum_tensor("p_st20", [1, 512], F32))
                p_mb = P0.enter_context(nc.psum_tensor("p_mb0", [128, 512], F32))
                p_rb = P0.enter_context(nc.psum_tensor("p_rb0", [128, 512], F32))
                t_prev = 0
                for tt in range(NT):
                    bld.wait("sync", "dve", t_prev)
                    t_x = bld.dma("sync", x_tt[:], x_own[tt], "dW")
                    t_prev = ln_tile(
                        x_tt,
                        lambda fc, tt=tt: xn[:, fc * TOK + tt * 512:fc * TOK + (tt + 1) * 512],
                        512, sq0, s10, p_st, p_st2, p_mb, p_rb, t_x, t_prev,
                        rows=rows_0)
                    bld.flush()
                bld.barrier()

            if debug:
                bld.dma("scalar", dbg["xn"], xn[:], "dY")
                bld.barrier()
            if phases <= 3:
                return

            with contextlib.ExitStack() as AS:
                v_all = AS.enter_context(nc.sbuf_tensor("v_all", [128, 8 * 16 * 128], BF16))
                wqk_st = AS.enter_context(nc.sbuf_tensor("wqk_st", [128, 1024], F32))
                wq_b = [AS.enter_context(nc.sbuf_tensor(f"wq_b{i}", [128, 2048], BF16))
                        for i in range(2)]
                wk_b2 = [AS.enter_context(nc.sbuf_tensor(f"wk_b2{i}", [128, 2048], BF16))
                         for i in range(2)]
                q_fm = [AS.enter_context(nc.sbuf_tensor(f"q_fm{i}", [128, TOK], BF16))
                        for i in range(2)]
                k_fm = [AS.enter_context(nc.sbuf_tensor(f"k_fm{i}", [128, TOK], BF16))
                        for i in range(2)]
                k_tm = [AS.enter_context(nc.sbuf_tensor(f"k_tm{i}", [128, TOK], BF16))
                        for i in range(2)]
                s1 = AS.enter_context(nc.sbuf_tensor("s1a", [128, 512], F32))
                s2 = AS.enter_context(nc.sbuf_tensor("s2a", [128, 512], F32))
                s3 = AS.enter_context(nc.sbuf_tensor("s3a", [128, 512], F32))
                attT_s = AS.enter_context(nc.sbuf_tensor("attT_s", [128, 384], BF16))
                den_s = AS.enter_context(nc.sbuf_tensor("den_s", [1, 256], F32))
                tri_s = AS.enter_context(nc.sbuf_tensor("tri_s", [128, 128], F32))
                S_bf = AS.enter_context(nc.sbuf_tensor("S_bf", [128, 129], BF16))
                rec_s = AS.enter_context(nc.sbuf_tensor("rec_s", [1, 256], F32))
                rec2_s = AS.enter_context(nc.sbuf_tensor("rec2_s", [1, 256], F32))
                rb_s = AS.enter_context(nc.sbuf_tensor("rb_s", [128, 256], F32))
                ao = [AS.enter_context(nc.sbuf_tensor(f"ao{i}", [128, TOK], BF16))
                      for i in range(2)]
                p_q = AS.enter_context(nc.psum_tensor("p_q", [128, 512], F32))
                p_kp = AS.enter_context(nc.psum_tensor("p_kp", [128, 512], F32))
                p_a = AS.enter_context(nc.psum_tensor("p_a", [128, 384], F32))
                p_n = AS.enter_context(nc.psum_tensor("p_n", [128, 256], F32))
                p_d = AS.enter_context(nc.psum_tensor("p_d", [1, 256], F32))
                p_r = AS.enter_context(nc.psum_tensor("p_r", [128, 256], F32))
                p_tr = AS.enter_context(nc.psum_tensor("p_tr2", [128, 256], BF16))
                p_su = AS.enter_context(nc.psum_tensor("p_su2", [128, 129], F32))

                bld.dma("sync", tri_s[:], tri_d, "dW")
                st = dict(vevac=0, vmm_last=0, qpsum_free=0, kpsum_free=0, kdone=0,
                          stage_dve=0, stage_act=0,
                          wq_used={0: 0, 1: 0}, wk_used={0: 0, 1: 0},
                          ktm_evac=0, q_bank_guard=0, cq={0: 0, 1: 0}, ck={0: 0, 1: 0},
                          lq={0: 0, 1: 0}, lk={0: 0, 1: 0})
                t_spill = {0: 0, 1: 0}

                def load_wqk(h):
                    par = h % 2
                    for pc in range(2):
                        bld.wait("sync", "dve", st["stage_dve"])
                        bld.wait("sync", "act", st["stage_act"])
                        st["lq"][par] = bld.dma("sync", wqk_st[:],
                                                wq_d[h][:, pc * 1024:(pc + 1) * 1024], "dW")
                        bld.wait("vector", "dW", st["lq"][par])
                        bld.wait("vector", "pe", st["wq_used"][par])
                        st["cq"][par] = bld.op("vector", lambda e, par=par, pc=pc: e.tensor_copy(
                            out=wq_b[par][:, pc * 1024:(pc + 1) * 1024], in_=wqk_st[:]),
                            inc="dve")
                        st["stage_dve"] = st["cq"][par]
                    for pc in range(2):
                        bld.wait("sync", "dve", st["stage_dve"])
                        bld.wait("sync", "act", st["stage_act"])
                        st["lk"][par] = bld.dma("sync", wqk_st[:],
                                                wk_d[h][:, pc * 1024:(pc + 1) * 1024], "dW")
                        bld.wait("scalar", "dW", st["lk"][par])
                        bld.wait("scalar", "pe", st["wk_used"][par])
                        st["ck"][par] = bld.op("scalar", lambda e, par=par, pc=pc: _cp(
                            e, wk_b2[par][:, pc * 1024:(pc + 1) * 1024], wqk_st[:]),
                            inc="act")
                        st["stage_act"] = st["ck"][par]

                def emit_trans_tt(h, tt):
                    par = h % 2
                    for ts4 in range(2):
                        bld.wait("tensor", "act", st["ktm_evac"])
                        t_trmm = None
                        for j in range(2):
                            sub = tt * 4 + ts4 * 2 + j
                            t_trmm = bld.op("tensor", lambda e, sub=sub, j=j, par=par: e.transpose(
                                out=p_tr[:, j * 128:(j + 1) * 128],
                                in_=k_fm[par][:, sub * 128:(sub + 1) * 128],
                                identity=ident_b[:]),
                                inc=("pe" if j == 1 else None))
                        bld.wait("scalar", "pe", t_trmm)
                        base = (tt * 4 + ts4 * 2) * 128
                        st["ktm_evac"] = bld.op("scalar", lambda e, base=base, par=par: e.copy(
                            out=k_tm[par][:, base:base + 256], in_=p_tr[:]), inc="act")

                def emit_proj_tt(h, tt, transposes=True, do_q=True, do_k=True):
                    par = h % 2
                    if do_q:
                        bld.wait("tensor", "dve", st["cq"][par])
                        bld.wait("tensor", "act", st["q_bank_guard"])
                        st["q_bank_guard"] = 0
                        bld.wait("tensor", "dve", st["qpsum_free"])
                        for fc in range(16):
                            t_qmm = bld.op("tensor", lambda e, tt=tt, fc=fc, par=par: e.matmul(
                                out=p_q[:], lhsT=wq_b[par][:, fc * 128:(fc + 1) * 128],
                                rhs=xn[:, fc * TOK + tt * 512:fc * TOK + (tt + 1) * 512],
                                start=(fc == 0), stop=(fc == 15)),
                                inc=("pe" if fc == 15 else None))
                        st["wq_used"][par] = t_qmm
                        bld.wait("vector", "pe", t_qmm)
                        st["qpsum_free"], _ = _rope_elu(
                            bld, p_q, cos_o[:, tt * 512:(tt + 1) * 512],
                            sin_o[0:64, tt * 512:(tt + 1) * 512],
                            sin_o[64:128, tt * 512:(tt + 1) * 512], s1, s2, s3,
                            q_fm[par][:, tt * 512:(tt + 1) * 512], 512)
                    if do_k:
                        bld.wait("tensor", "act", st["ck"][par])
                        bld.wait("tensor", "dve", st["kpsum_free"])
                        for fc in range(16):
                            t_kmm = bld.op("tensor", lambda e, tt=tt, fc=fc, par=par: e.matmul(
                                out=p_kp[:], lhsT=wk_b2[par][:, fc * 128:(fc + 1) * 128],
                                rhs=xn[:, fc * TOK + tt * 512:fc * TOK + (tt + 1) * 512],
                                start=(fc == 0), stop=(fc == 15)),
                                inc=("pe" if fc == 15 else None))
                        st["wk_used"][par] = t_kmm
                        bld.wait("vector", "pe", t_kmm)
                        st["kpsum_free"], t_kdone = _rope_elu(
                            bld, p_kp, cos_o[:, tt * 512:(tt + 1) * 512],
                            sin_o[0:64, tt * 512:(tt + 1) * 512],
                            sin_o[64:128, tt * 512:(tt + 1) * 512], s1, s2, s3,
                            k_fm[par][:, tt * 512:(tt + 1) * 512], 512)
                        st["kdone"] = t_kdone
                        if transposes:
                            bld.wait("tensor", "dve", t_kdone)
                            emit_trans_tt(h, tt)

                for grp in range(A_GRP):
                    with nc.sbuf_tensor(f"wv_bg{grp}", [128, 8192], BF16) as wv_b, \
                         nc.sbuf_tensor(f"wv_stg{grp}", [128, 1024], F32) as wv_st:
                        for hgl in range(2):
                            hg = grp * 2 + hgl
                            bld.wait("sync", "pe", st["vmm_last"])
                            t_cast = 0
                            for pc in range(8):
                                bld.wait("sync", "dve", t_cast)
                                t_l = bld.dma("sync", wv_st[:],
                                              wv_d[hg][:, pc * 1024:(pc + 1) * 1024], "dW")
                                bld.wait("vector", "dW", t_l)
                                t_cast = bld.op("vector", lambda e, pc=pc: e.tensor_copy(
                                    out=wv_b[:, pc * 1024:(pc + 1) * 1024], in_=wv_st[:]),
                                    inc="dve")
                            for ts in range(16):
                                bld.wait("tensor", "dve", t_cast)
                                bld.wait("tensor", "act", st["vevac"])
                                bld.wait("tensor", "dve", st["qpsum_free"])
                                for fc in range(16):
                                    st["vmm_last"] = bld.op("tensor", lambda e, ts=ts, fc=fc: e.matmul(
                                        out=p_q[:],
                                        lhsT=xn[:, fc * TOK + ts * 128:fc * TOK + ts * 128 + 128],
                                        rhs=wv_b[:, fc * 512:(fc + 1) * 512],
                                        start=(fc == 0), stop=(fc == 15)),
                                        inc=("pe" if fc == 15 else None))
                                bld.wait("scalar", "pe", st["vmm_last"])
                                st["vevac"] = bld.op("scalar", lambda e, hgl=hgl, ts=ts: e.copy(
                                    out=v_all[:, (hgl * 16 + ts) * 512:(hgl * 16 + ts + 1) * 512],
                                    in_=p_q[:]), inc="act")
                            st["qpsum_free"] = 0
                        bld.flush()
                    st["q_bank_guard"] = st["vevac"]
                    load_wqk(grp * 8)
                    for tt in range(NT):
                        emit_proj_tt(grp * 8, tt)
                    bld.flush()
                    for hl in range(A_HEADS):
                        h = grp * 8 + hl
                        par = h % 2
                        aob = ao[par]
                        qf, kf, ktm_ = q_fm[par], k_fm[par], k_tm[par]
                        if hl < 7:
                            load_wqk(h + 1)
                            if NO_ILV:
                                for tt in range(NT):
                                    emit_proj_tt(h + 1, tt)
                        if not NO_ILV and hl > 0:
                            if ILV_K == 0:
                                for _tt in range(NT):
                                    emit_proj_tt(h, _tt, transposes=False,
                                                 do_q=False, do_k=True)
                            elif ILV_K == 2:
                                for _tt in range(NT):
                                    emit_proj_tt(h, _tt, transposes=False,
                                                 do_q=True, do_k=False)
                            bld.wait("tensor", "dve", st["kdone"])
                            for _tt in range(NT):
                                emit_trans_tt(h, _tt)
                        bld.wait("scalar", "dve", bld.sem["dve"].n)
                        t_sbf = bld.op("scalar", lambda e, h=h: e.copy(
                            out=S_bf[:], in_=S_all[:, h * 129:(h + 1) * 129]), inc="act")
                        bld.wait("vector", "dY", t_spill[par])
                        t_evD = t_evA = t_denf = t_rec = t_rbev = t_out = 0
                        t_numm = t_summ = t_sadd = t_denm = t_rbmm = 0
                        for c in range(A_CHUNKS):
                            qs256 = qf[:, c * 256:(c + 1) * 256]
                            ks_m0 = kf[:, c * 256:c * 256 + 128]
                            ks_m1 = kf[:, c * 256 + 128:(c + 1) * 256]
                            qs_l1 = qf[:, c * 256 + 128:(c + 1) * 256]
                            bld.wait("tensor", "dve", t_evD)
                            bld.wait("tensor", "act", t_evA)
                            bld.op("tensor", lambda e, a=ks_m0, b=qs256: e.matmul(
                                out=p_a[:, 0:256], lhsT=a, rhs=b, start=True, stop=False))
                            t_att = bld.op("tensor", lambda e, a=ks_m1, b=qs_l1: e.matmul(
                                out=p_a[:, 256:384], lhsT=a, rhs=b, start=False, stop=True),
                                inc="pe")
                            bld.wait("vector", "pe", t_att)
                            bld.op("vector", lambda e: e.tensor_tensor(
                                out=attT_s[:, 0:128], in0=p_a[:, 0:128], in1=tri_s[:],
                                op=ALU.mult))
                            t_evD = bld.op("vector", lambda e: e.tensor_tensor(
                                out=attT_s[:, 256:384], in0=p_a[:, 256:384], in1=tri_s[:],
                                op=ALU.mult), inc="dve")
                            bld.wait("scalar", "pe", t_att)
                            t_evA = bld.op("scalar", lambda e: e.copy(
                                out=attT_s[:, 128:256], in_=p_a[:, 128:256]), inc="act")
                            bld.wait("tensor", "dve", t_evD)
                            bld.wait("tensor", "act", t_evA)
                            bld.wait("tensor", "act", t_sbf)
                            bld.wait("tensor", "act", st["vevac"])
                            bld.wait("tensor", "dve", t_out)

                            def _voff(sub, hl=hl):
                                return ((hl // 4) * 16 + sub) * 512 + (hl % 4) * 128
                            v0o, v1o = _voff(2 * c), _voff(2 * c + 1)
                            bld.op("tensor", lambda e, v0o=v0o: e.matmul(
                                out=p_n[:, 0:256], lhsT=v_all[:, v0o:v0o + 128],
                                rhs=attT_s[:, 0:256], start=True, stop=False))
                            bld.op("tensor", lambda e, v1o=v1o: e.matmul(
                                out=p_n[:, 128:256], lhsT=v_all[:, v1o:v1o + 128],
                                rhs=attT_s[:, 256:384], start=False, stop=False))
                            t_numm = bld.op("tensor", lambda e, b=qs256: e.matmul(
                                out=p_n[:, 0:256], lhsT=S_bf[:, 0:128], rhs=b,
                                start=False, stop=True), inc="pe")
                            bld.wait("tensor", "dve", t_denf)
                            bld.op("tensor", lambda e: e.matmul(
                                out=p_d[:, 0:256], lhsT=onescol_b[:], rhs=attT_s[:, 0:256],
                                start=True, stop=False))
                            bld.op("tensor", lambda e: e.matmul(
                                out=p_d[:, 128:256], lhsT=onescol_b[:], rhs=attT_s[:, 256:384],
                                start=False, stop=False))
                            t_denm = bld.op("tensor", lambda e, b=qs256: e.matmul(
                                out=p_d[:, 0:256], lhsT=S_bf[:, 128:129], rhs=b,
                                start=False, stop=True), inc="pe")
                            bld.wait("tensor", "act", st["ktm_evac"])
                            bld.wait("tensor", "dve", t_sadd)
                            for j in range(2):
                                sub = 2 * c + j
                                kslT = ktm_[:, sub * 128:(sub + 1) * 128]
                                vo = _voff(sub)
                                vsl = v_all[:, vo:vo + 128]
                                bld.op("tensor", lambda e, a=kslT, b=vsl, j=j: e.matmul(
                                    out=p_su[:, 0:128], lhsT=a, rhs=b,
                                    start=(j == 0), stop=False))
                                t_summ = bld.op("tensor", lambda e, a=kslT, j=j: e.matmul(
                                    out=p_su[:, 128:129], lhsT=a, rhs=onescol_b[:],
                                    start=False, stop=(j == 1)),
                                    inc=("pe" if j == 1 else None))
                            bld.wait("vector", "pe", t_denm)
                            bld.wait("vector", "act", t_rec)
                            t_denf = bld.op("vector", lambda e: e.tensor_scalar_max(
                                out=den_s[:], in0=p_d[:], scalar1=1e-6), inc="dve")
                            bld.wait("scalar", "dve", t_denf)
                            bld.wait("scalar", "pe", t_rbmm)
                            bld.op("scalar", lambda e: e.activation(
                                out=rec_s[:], in_=den_s[:], func=AF.Ln))
                            t_rec = bld.op("scalar", lambda e: e.activation(
                                out=rec2_s[:], in_=rec_s[:], func=AF.Exp, scale=-1.0),
                                inc="act")
                            bld.wait("vector", "pe", t_summ)
                            bld.wait("vector", "act", t_sbf)
                            t_sadd = bld.op("vector", lambda e, h=h: e.tensor_tensor(
                                out=S_all[:, h * 129:(h + 1) * 129],
                                in0=S_all[:, h * 129:(h + 1) * 129],
                                in1=p_su[:], op=ALU.add), inc="dve")
                            bld.wait("scalar", "dve", t_sadd)
                            t_sbf = bld.op("scalar", lambda e, h=h: e.copy(
                                out=S_bf[:], in_=S_all[:, h * 129:(h + 1) * 129]), inc="act")
                            bld.wait("tensor", "act", t_rec)
                            bld.wait("tensor", "act", t_rbev)
                            t_rbmm = bld.op("tensor", lambda e: e.matmul(
                                out=p_r[:], lhsT=ones_s[0:1, :], rhs=rec2_s[0:1, :],
                                start=True, stop=True), inc="pe")
                            bld.wait("scalar", "pe", t_rbmm)
                            t_rbev = bld.op("scalar", lambda e: e.copy(
                                out=rb_s[:], in_=p_r[:]), inc="act")
                            bld.wait("vector", "pe", t_numm)
                            bld.wait("vector", "act", t_rbev)
                            t_out = bld.op("vector", lambda e, c=c, aob=aob: e.tensor_tensor(
                                out=aob[:, c * 256:(c + 1) * 256], in0=p_n[:],
                                in1=rb_s[:], op=ALU.mult), inc="dve")
                            if hl < 7 and not NO_ILV and c == ILV_AT:
                                for _tt in range(NT):
                                    emit_proj_tt(h + 1, _tt, transposes=False,
                                                 do_q=(ILV_K != 2), do_k=(ILV_K in (1, 2)))
                        bld.wait("scalar", "dve", t_out)
                        t_sp = 0
                        for tt in range(NT):
                            t_sp = bld.dma("scalar", sp_attn[tt][h],
                                           aob[:, tt * 512:(tt + 1) * 512], "dY")
                        t_spill[par] = t_sp
                        if debug and h == 0:
                            bld.dma("scalar", dbg["q0"], qf[:], "dY")
                            bld.dma("scalar", dbg["k0"], kf[:], "dY")
                            bld.dma("scalar", dbg["att0"], aob[:], "dY")
                            t_dbg = bld.dma("scalar", dbg["v0"], v_all[:, 0:2048], "dY")
                            for eng in ("tensor", "vector", "scalar", "sync", "gpsimd"):
                                bld.wait(eng, "dY", t_dbg)
                        bld.flush()
                bld.barrier()
            if debug:
                for tt in range(NT):
                    for hh in range(H):
                        bld.dma("scalar",
                                dbg["attnall"][:, hh * TOK + tt * 512:hh * TOK + (tt + 1) * 512],
                                sp_attn[tt][hh], "dY")
                bld.barrier()

            with contextlib.ExitStack() as FS:
                wgu_st = [FS.enter_context(nc.sbuf_tensor(f"wgu_st{i}", [128, 4096], F32))
                          for i in range(2)]
                wgu_b = [FS.enter_context(nc.sbuf_tensor(f"wgu_b{i}", [128, 4096], BF16))
                         for i in range(2)]
                gated = [FS.enter_context(nc.sbuf_tensor(f"gated{i}", [128, TOK], BF16))
                         for i in range(2)]
                sg_s = FS.enter_context(nc.sbuf_tensor("sg_s", [128, 2 * 512], F32))
                wpp_st = [FS.enter_context(nc.sbuf_tensor(f"wpp_st{i}", [128, 2560], F32))
                          for i in range(2)]
                wpp_bf = [FS.enter_context(nc.sbuf_tensor(f"wpp_bf{i}", [128, 2560], BF16))
                          for i in range(2)]
                p_g = [FS.enter_context(nc.psum_tensor(f"p_g{i}", [128, 512], F32))
                       for i in range(2)]
                p_u = [FS.enter_context(nc.psum_tensor(f"p_u{i}", [128, 512], F32))
                       for i in range(2)]
                t_cast_par = {0: 0, 1: 0}
                t_lastmm_par = {0: 0, 1: 0}
                t_gev = {}
                t_spill = {0: 0, 1: 0}
                def emit_wpp_piece(ocp, pc):
                    par = pc % 2
                    eng, semn = (("vector", "dve") if par == 0 else ("scalar", "act"))
                    bld.wait("sync", semn, bld.t_wppc.get(par, 0))
                    t_l = bld.dma("sync", wpp_st[par][:],
                                  wdo_d[ocp][:, pc * 2560:(pc + 1) * 2560], "dW")
                    bld.wait(eng, "dW", t_l)
                    bld.wait(eng, "dY", bld.t_wpp.get(par, 0))
                    bld.t_wppc[par] = bld.op(eng, lambda e, par=par: _cp(
                        e, wpp_bf[par][:], wpp_st[par][:]), inc=semn)
                    bld.wait("scalar", semn, bld.t_wppc[par])
                    bld.t_wpp[par] = bld.dma(
                        "scalar", wdo_bf_d[ocp][:, pc * 2560:(pc + 1) * 2560],
                        wpp_bf[par][:], "dY")

                for hc in range(64):
                    emit_wpp_piece(hc // 4, hc % 4)
                    par = hc % 2
                    stb, gb, gtb = wgu_st[par], wgu_b[par], gated[par]
                    eng, semn = (("vector", "dve") if par == 0 else ("scalar", "act"))
                    bld.wait("sync", semn, t_cast_par[par])
                    t_l = bld.dma("sync", stb[:], wgu_d[hc], "dW")
                    bld.wait(eng, "dW", t_l)
                    bld.wait(eng, "pe", t_lastmm_par[par])
                    t_cast_par[par] = bld.op(eng, lambda e, stb=stb, gb=gb: _cp(
                        e, gb[:], stb[:]), inc=semn)
                    t_umm = 0
                    for tt in range(NT):
                        pg, pu = p_g[tt % 2], p_u[tt % 2]
                        bld.wait("tensor", semn, t_cast_par[par])
                        bld.wait("tensor", "act", t_gev.get((tt % 2, 0), 0))
                        bld.wait("tensor", "dve", t_gev.get((tt % 2, 1), 0))
                        for fc in range(16):
                            bld.op("tensor", lambda e, fc=fc, tt=tt, pg=pg, gb=gb: e.matmul(
                                out=pg[:], lhsT=gb[:, fc * 128:(fc + 1) * 128],
                                rhs=xn[:, fc * TOK + tt * 512:fc * TOK + (tt + 1) * 512],
                                start=(fc == 0), stop=(fc == 15)))
                        for fc in range(16):
                            t_umm = bld.op("tensor", lambda e, fc=fc, tt=tt, pu=pu, gb=gb: e.matmul(
                                out=pu[:], lhsT=gb[:, 2048 + fc * 128:2048 + (fc + 1) * 128],
                                rhs=xn[:, fc * TOK + tt * 512:fc * TOK + (tt + 1) * 512],
                                start=(fc == 0), stop=(fc == 15)),
                                inc=("pe" if fc == 15 else None))
                        bld.wait("scalar", "pe", t_umm)
                        bld.wait("scalar", "dve", t_gev.get((tt % 2, 1), 0))
                        t_si = bld.op("scalar", lambda e, tt=tt, pg=pg: e.activation(
                            out=sg_s[:, (tt % 2) * 512:(tt % 2 + 1) * 512], in_=pg[:],
                            func=AF.Silu), inc="act")
                        t_gev[(tt % 2, 0)] = t_si
                        bld.wait("vector", "act", t_si)
                        bld.wait("vector", "dY", t_spill[par])
                        t_gv = bld.op("vector", lambda e, tt=tt, pu=pu, gtb=gtb: e.tensor_tensor(
                            out=gtb[:, tt * 512:(tt + 1) * 512],
                            in0=sg_s[:, (tt % 2) * 512:(tt % 2 + 1) * 512],
                            in1=pu[:], op=ALU.mult), inc="dve")
                        t_gev[(tt % 2, 1)] = t_gv
                    t_lastmm_par[par] = t_umm
                    bld.wait("scalar", "dve", t_gv)
                    for tt in range(NT):
                        t_sp = bld.dma("scalar", sp_gated[tt][hc],
                                       gtb[:, tt * 512:(tt + 1) * 512], "dY")
                    t_spill[par] = t_sp
                    if debug and hc == 0:
                        bld.dma("scalar", dbg["gated0"], gtb[:], "dY")
                    bld.flush()
                bld.barrier()
        _main_scope()
        if phases <= 5:
            raise _Cut

        SS.close()
        with contextlib.ExitStack() as ZS:
            g_t = [ZS.enter_context(nc.sbuf_tensor(f"g_t{i}", [128, 64 * 512], BF16))
                   for i in range(2)]
            a_t = [ZS.enter_context(nc.sbuf_tensor(f"a_t{i}", [128, 16 * 512], BF16))
                   for i in range(2)]
            wdo_b = [ZS.enter_context(nc.sbuf_tensor(f"wdo_b{i}", [128, 10240], BF16))
                     for i in range(2)]
            x_bl = [ZS.enter_context(nc.sbuf_tensor("x_bl0", [128, 512], F32))] * 2
            y_s = [ZS.enter_context(nc.sbuf_tensor(f"y_s{i}", [128, 512], F32))
                   for i in range(2)]
            p_y = [ZS.enter_context(nc.psum_tensor(f"p_y{i}", [128, 512], F32))
                   for i in range(2)]
            t_wld = {0: 0, 1: 0}
            t_wlast = {0: 0, 1: 0}
            t_glast = {0: 0, 1: 0}
            t_yev = {0: 0, 1: 0}
            t_ydma = {0: 0, 1: 0}
            t_xl = {0: 0, 1: 0}

            def load_tile(tt):
                par = tt % 2
                bld.wait("sync", "pe", t_glast[par])
                bld.dma("sync", g_t[par][:].rearrange("p (c t) -> p c t", c=64),
                        sp_gated[tt].rearrange("c p t -> p c t"), "dW")
                return bld.dma("sync", a_t[par][:].rearrange("p (c t) -> p c t", c=16),
                               sp_attn[tt].rearrange("c p t -> p c t"), "dW")

            t_tile = {0: load_tile(0), 1: load_tile(1)}
            if debug:
                bld.wait("scalar", "dW", t_tile[0])
                bld.dma("scalar", dbg["wdo0"], wdo_bf_d[0], "dY")
                bld.dma("scalar", dbg["gt0"], g_t[0][:], "dY")
                bld.dma("scalar", dbg["at0"], a_t[0][:], "dY")
            bld.barrier()
            for half in range(2):
                for oc in range(16):
                    opar = oc % 2
                    wb = wdo_b[opar]
                    bld.wait("sync", "pe", t_wlast[opar])
                    t_wld[opar] = bld.dma("sync", wb[:], wdo_bf_d[oc], "dW")
                    for tt2 in range(2):
                        tt = half * 2 + tt2
                        par = tt % 2
                        gb, ab = g_t[par], a_t[par]
                        bld.wait("scalar", "dve", max(t_yev[0], t_yev[1]))
                        t_xl[opar] = bld.dma("scalar", x_bl[opar][:],
                                             x_own[tt][:, oc * 512:(oc + 1) * 512], "dX")
                        py = p_y[opar]
                        bld.wait("tensor", "dW", t_tile[par])
                        bld.wait("tensor", "dW", t_wld[opar])
                        bld.wait("tensor", "dve", t_yev[opar])
                        for kc in range(64):
                            bld.op("tensor", lambda e, kc=kc, py=py, wb=wb, gb=gb: e.matmul(
                                out=py[:], lhsT=wb[:, kc * 128:(kc + 1) * 128],
                                rhs=gb[:, kc * 512:(kc + 1) * 512],
                                start=(kc == 0), stop=False))
                        for ec in range(16):
                            t_ymm = bld.op("tensor", lambda e, ec=ec, py=py, wb=wb, ab=ab: e.matmul(
                                out=py[:], lhsT=wb[:, (64 + ec) * 128:(64 + ec + 1) * 128],
                                rhs=ab[:, ec * 512:(ec + 1) * 512],
                                start=False, stop=(ec == 15)),
                                inc=("pe" if ec == 15 else None))
                        t_wlast[opar] = t_ymm
                        t_glast[par] = t_ymm
                        bld.wait("vector", "pe", t_ymm)
                        bld.wait("vector", "dX", t_xl[opar])
                        bld.wait("vector", "dY", t_ydma[opar])
                        t_yev[opar] = bld.op("vector", lambda e, opar=opar, py=py: e.tensor_tensor(
                            out=y_s[opar][:], in0=x_bl[opar][:], in1=py[:], op=ALU.add),
                            inc="dve")
                        bld.wait("scalar", "dve", t_yev[opar])
                        t_ydma[opar] = bld.dma("scalar",
                                               y_out[oc][:, tt * 512:(tt + 1) * 512],
                                               y_s[opar][:], "dY")
                    bld.flush()
                if half == 0:
                    t_tile[0] = load_tile(2)
                    t_tile[1] = load_tile(3)
            bld.wait("scalar", "dY", bld.sem["dY"].n)
            bld.barrier()

    except _Cut:
        pass
    es.close()
    return nc



_NC_CACHE = {}


def kernel(**inputs):
    in_maps = _stage(inputs)
    key = "dbg" if DEBUG else "rel"
    if key not in _NC_CACHE:
        key = (key, PHASES)
    if key not in _NC_CACHE:
        _NC_CACHE[key] = build_nc(debug=DEBUG)
    nc = _NC_CACHE[key]
    res = run_bass_kernel_spmd(nc, in_maps, core_ids=list(range(NCORES)))
    out = np.empty((B, S, E), np.float32)
    for c in range(NCORES):
        b, half = c // 2, c % 2
        yc = np.asarray(res.results[c]["y"])
        out[b, half * TOK:(half + 1) * TOK, :] = (
            yc.reshape(16, 128, TOK).transpose(2, 0, 1).reshape(TOK, E))
    if DEBUG:
        kernel._dbg = res.results
    return out


# version 19
# speedup vs baseline: 1.1923x; 1.1923x over previous
import os
import contextlib
import numpy as np
import concourse.bass as bass
import concourse.mybir as mybir
from concourse.bass_utils import run_bass_kernel_spmd

F32 = mybir.dt.float32
BF16 = mybir.dt.bfloat16
AF = mybir.ActivationFunctionType
ALU = mybir.AluOpType

B, S, E, H, D, CH, HID = 4, 4096, 2048, 16, 128, 256, 8192
TOK = 2048
NT = 4
NTP = 8
NCH = 8
ROPE_BASE = 10000.0
PERM = np.concatenate([np.arange(0, D, 2), np.arange(1, D, 2)])
NCORES = 8

DEBUG = os.environ.get("ATHENA_KERNEL_DEBUG", "0") == "1"
PHASES = int(os.environ.get("ATHENA_KERNEL_PHASES", "6"))
A_GRP = int(os.environ.get("ATHENA_A_GRP", "2"))
A_HEADS = int(os.environ.get("ATHENA_A_HEADS", "8"))
A_CHUNKS = int(os.environ.get("ATHENA_A_CHUNKS", "8"))
NO_ILV = os.environ.get("ATHENA_NO_ILV", "0") == "1"
ILV_AT = int(os.environ.get("ATHENA_ILV_AT", "1"))
ILV_K = int(os.environ.get("ATHENA_ILV_K", "1"))



def _rope_tables(pos0):
    invf = 1.0 / ROPE_BASE ** (np.arange(0, D, 2, dtype=np.float64) / D)
    pos = np.arange(pos0, pos0 + TOK, dtype=np.float64)
    freqs = np.outer(pos, invf)
    emb = np.concatenate([freqs, freqs], -1)
    cos = np.ascontiguousarray(np.cos(emb).astype(np.float32)[:, PERM].T)
    sin = np.sin(emb).astype(np.float32)[:, PERM].T.copy()
    sin[:64, :] *= -1.0
    return cos, np.ascontiguousarray(sin)


def _stage(inputs):
    x = np.asarray(inputs["L0_attn_x"], dtype=np.float32)
    Wqkv = np.asarray(inputs["L0_attn_W_qkv"], dtype=np.float32)
    Wout = np.asarray(inputs["L0_attn_W_out"], dtype=np.float32)
    Wg = np.asarray(inputs["L0_ffn_w_gate"], dtype=np.float32)
    Wu = np.asarray(inputs["L0_ffn_w_up"], dtype=np.float32)
    Wd = np.asarray(inputs["L0_ffn_w_down"], dtype=np.float32)
    Wq, Wk, Wv = Wqkv[0:E], Wqkv[E:2 * E], Wqkv[2 * E:3 * E]

    def lhsT_headed(W, nh, perm=None):
        A = W.reshape(nh, 128, 16, 128).transpose(0, 3, 2, 1)
        if perm is not None:
            A = A[:, :, :, perm]
        return np.ascontiguousarray(A).reshape(nh, 128, 16 * 128)

    wq = lhsT_headed(Wq, H, PERM)
    wk = lhsT_headed(Wk, H, PERM)
    wv = np.ascontiguousarray(
        Wv.reshape(4, 512, 16, 128).transpose(0, 3, 2, 1)).reshape(4, 128, 8192)
    wg = Wg.reshape(64, 128, 16, 128).transpose(0, 3, 2, 1).reshape(64, 128, 2048)
    wu = Wu.reshape(64, 128, 16, 128).transpose(0, 3, 2, 1).reshape(64, 128, 2048)
    wgu = np.ascontiguousarray(np.concatenate([wg, wu], axis=2))
    wd = Wd.reshape(16, 128, 64, 128).transpose(0, 3, 2, 1).reshape(16, 128, 8192)
    wo = Wout.reshape(16, 128, 16, 128).transpose(0, 3, 2, 1).reshape(16, 128, 2048)
    wdo = np.ascontiguousarray(np.concatenate([wd, wo], axis=2))

    x_fm = np.ascontiguousarray(x.transpose(0, 2, 1))

    def stage_x(b, t0, ntile, tl):
        sl = x_fm[b][:, t0:t0 + ntile * tl]
        A = sl.reshape(16, 128, ntile, tl).transpose(2, 1, 0, 3)
        return np.ascontiguousarray(A).reshape(ntile, 128, 16 * tl)

    cos0, sin0 = _rope_tables(0)
    cos1, sin1 = _rope_tables(TOK)
    ones2d = np.ones((128, 128), np.float32)
    tri = (np.arange(128)[None, :] >= np.arange(128)[:, None]).astype(np.float32)
    ident = np.eye(128, dtype=np.float32)
    zpre = np.zeros((NTP, 128, 16 * 256), np.float32)

    shared = dict(wq=wq, wk=wk, wv=wv, wgu=wgu, wdo=wdo, ones2d=ones2d,
                  tri=tri, ident=ident, cos_pre=cos0, sins_pre=sin0)
    in_maps = []
    for c in range(NCORES):
        b, half = c // 2, c % 2
        m = dict(shared)
        m["x_own"] = stage_x(b, half * TOK, NT, 512)
        m["x_pre"] = stage_x(b, 0, NTP, 256) if half == 1 else zpre
        m["cos_own"] = cos1 if half == 1 else cos0
        m["sins_own"] = sin1 if half == 1 else sin0
        m["aug"] = np.full((128, 1), float(half), np.float32)
        in_maps.append(m)
    return in_maps



class _Cut(Exception):
    pass


class Sem:
    def __init__(self, h):
        self.h = h
        self.n = 0

    def inc(self, k=1):
        self.n += k
        return self.n


class Bld:
    ENGS = ("sync", "tensor", "vector", "scalar", "gpsimd")

    def __init__(self, nc, block, es):
        self.nc = nc
        self.block = block
        self.q = {e: [] for e in self.ENGS}
        self.sem = {n: Sem(es.enter_context(nc.semaphore(n)))
                    for n in ("dW", "dX", "dY", "pe", "dve", "act", "pool", "bar")}
        self.rope_prev_exp = 0
        self.t_wpp = {}
        self.t_wppc = {}

    def wait(self, eng, sem, val):
        if not val or val <= 0:
            return
        h = self.sem[sem].h
        self.q[eng].append(lambda e, h=h, v=val: e.wait_ge(h, v))

    def op(self, eng, fn, inc=None):
        if inc is None:
            self.q[eng].append(lambda e, f=fn: f(e))
            return None
        name, k = (inc, 1) if isinstance(inc, str) else inc
        s = self.sem[name]
        tgt = s.inc(k)
        self.q[eng].append(lambda e, f=fn, h=s.h, k=k: f(e).then_inc(h, k))
        return tgt

    def dma(self, eng, dst, src, sem):
        return self.op(eng, lambda e, d=dst, s=src: e.dma_start(out=d, in_=s),
                       (sem, 16))

    def flush(self):
        for eng in self.ENGS:
            fns = self.q[eng]
            if not fns:
                continue
            self.q[eng] = []
            getattr(self.block, eng)(lambda e, fns=fns: [f(e) for f in fns])

    def barrier(self):
        self.wait("sync", "dW", self.sem["dW"].n)
        self.wait("scalar", "dX", self.sem["dX"].n)
        self.wait("scalar", "dY", self.sem["dY"].n)
        bar = self.sem["bar"]
        for eng in self.ENGS:
            self.q[eng].append(
                lambda e, h=bar.h: (e.drain(), e.sem_inc(h, 1)))
        bar.n += 5
        tgt = bar.n
        for eng in self.ENGS:
            self.wait(eng, "bar", tgt)
        self.flush()



def _cp(e, out, in_):
    if hasattr(e, "tensor_copy"):
        return e.tensor_copy(out=out, in_=in_)
    return e.copy(out=out, in_=in_)

def _rope_elu(bld, p_src, cos_ap, sins_lo, sins_hi, s1, s2, s3, dst, width):
    bld.op("vector", lambda e: e.tensor_tensor(
        out=s1[:, 0:width], in0=p_src[:, 0:width], in1=cos_ap, op=ALU.mult))
    bld.op("vector", lambda e: e.tensor_tensor(
        out=s2[0:64, 0:width], in0=p_src[64:128, 0:width], in1=sins_lo,
        op=ALU.mult))
    t_psum_done = bld.op("vector", lambda e: e.tensor_tensor(
        out=s2[64:128, 0:width], in0=p_src[0:64, 0:width], in1=sins_hi,
        op=ALU.mult), inc="dve")
    bld.wait("vector", "act", bld.rope_prev_exp)
    bld.op("vector", lambda e: e.tensor_tensor(
        out=s3[:, 0:width], in0=s1[:, 0:width], in1=s2[:, 0:width], op=ALU.add))
    t_pool = bld.op("vector", lambda e: e.tensor_scalar_min(
        out=s1[:, 0:width], in0=s3[:, 0:width], scalar1=0.0), inc="dve")
    bld.wait("scalar", "dve", t_pool)
    t_exp = bld.op("scalar", lambda e: e.activation(
        out=s2[:, 0:width], in_=s1[:, 0:width], func=AF.Exp), inc="act")
    bld.rope_prev_exp = t_exp
    bld.wait("vector", "act", t_exp)
    t_done = bld.op("vector", lambda e: e.scalar_tensor_tensor(
        out=dst, in0=s3[:, 0:width], scalar=0.0, in1=s2[:, 0:width],
        op0=ALU.max, op1=ALU.add), inc="dve")
    return t_psum_done, t_done



def build_nc(debug=False, phases=None):
    if phases is None:
        phases = PHASES
    nc = bass.Bass("TRN2", target_bir_lowering=False)

    def din(name, shape, dt=F32):
        return nc.dram_tensor(name, shape, dt, kind="ExternalInput").ap()

    x_own = din("x_own", [NT, 128, 16 * 512])
    x_pre = din("x_pre", [NTP, 128, 16 * 256])
    wq_d = din("wq", [H, 128, E])
    wk_d = din("wk", [H, 128, E])
    wv_d = din("wv", [4, 128, 8192])
    wgu_d = din("wgu", [64, 128, 4096])
    wdo_d = din("wdo", [16, 128, 10240])
    ones_d = din("ones2d", [128, 128])
    tri_d = din("tri", [128, 128])
    ident_d = din("ident", [128, 128])
    cos_own_d = din("cos_own", [128, TOK])
    sins_own_d = din("sins_own", [128, TOK])
    cos_pre_d = din("cos_pre", [128, TOK])
    sins_pre_d = din("sins_pre", [128, TOK])
    aug_d = din("aug", [128, 1])

    y_out = nc.dram_tensor("y", [16, 128, TOK], F32, kind="ExternalOutput").ap()
    sp_attn = nc.dram_tensor("sp_attn", [NT, H, 128, 512], BF16, kind="Internal").ap()
    sp_gated = nc.dram_tensor("sp_gated", [NT, 64, 128, 512], BF16, kind="Internal").ap()
    wdo_bf_d = nc.dram_tensor("wdo_bf", [16, 128, 10240], BF16, kind="Internal").ap()

    dbg = {}
    if debug:
        for nm, shp, dt in (("xn", [128, 16 * TOK], BF16), ("S", [128, H * 129], F32),
                            ("q0", [128, TOK], BF16), ("k0", [128, TOK], BF16),
                            ("att0", [128, TOK], BF16), ("v0", [128, 2048], BF16),
                            ("gated0", [128, TOK], BF16), ("wdo0", [128, 10240], BF16),
                            ("attnall", [128, H * TOK], BF16),
                            ("gt0", [128, 64 * 512], BF16), ("at0", [128, 16 * 512], BF16)):
            dbg[nm] = nc.dram_tensor("dbg_" + nm, shp, dt, kind="ExternalOutput").ap()

    es = contextlib.ExitStack()
    block = es.enter_context(nc.Block())
    SS = es.enter_context(contextlib.ExitStack())
    ones_s = SS.enter_context(nc.sbuf_tensor("ones_s", [128, 128], F32))
    ident_b = SS.enter_context(nc.sbuf_tensor("ident_b", [128, 128], BF16))
    onescol_b = SS.enter_context(nc.sbuf_tensor("onescol_b", [128, 1], BF16))
    aug_b = SS.enter_context(nc.sbuf_tensor("aug_b", [128, 1], BF16))
    S_all = SS.enter_context(nc.sbuf_tensor("S_all", [128, H * 129], F32))
    bld = Bld(nc, block, es)

    try:
        with nc.sbuf_tensor("cst", [128, 128], F32) as cst, \
             nc.sbuf_tensor("cst2", [128, 1], F32) as cst2:
            bld.dma("sync", ones_s[:], ones_d, "dW")
            bld.dma("sync", cst[:], ident_d, "dW")
            t_l = bld.dma("sync", cst2[:], aug_d, "dW")
            bld.wait("vector", "dW", t_l)
            bld.op("vector", lambda e: e.tensor_copy(out=ident_b[:], in_=cst[:]))
            bld.op("vector", lambda e: e.tensor_copy(out=onescol_b[:], in_=ones_s[:, 0:1]))
            bld.op("vector", lambda e: e.tensor_copy(out=aug_b[:], in_=cst2[:]))
            bld.op("vector", lambda e: e.memset(S_all[:], 0.0), inc="dve")
            bld.barrier()
        if phases <= 1:
            raise _Cut

        def ln_tile(x_s, xn_dst16, TL, sq_s, s_sub, p_st, p_st2, p_mb, p_rb,
                    t_xload, t_prev_done, rows=None):
            row_m, row_b, row_r, row_c = rows
            """Returns dve target when all xn writes (and all x_s reads) done."""
            bld.wait("tensor", "dW", t_xload)
            bld.wait("tensor", "dve", t_prev_done)
            for fc in range(16):
                bld.op("tensor", lambda e, fc=fc: e.matmul(
                    out=p_st[:], lhsT=ones_s[:, 0:1], rhs=x_s[:, fc * TL:(fc + 1) * TL],
                    start=(fc == 0), stop=(fc == 15)))
            bld.wait("scalar", "dW", t_xload)
            bld.wait("scalar", "dve", t_prev_done)
            t_mm = [0] * 16
            for fc in range(16):
                if fc >= 2:
                    bld.wait("scalar", "pe", t_mm[fc - 2])
                t_sq = bld.op("scalar", lambda e, fc=fc: e.activation(
                    out=sq_s[:, (fc % 2) * TL:(fc % 2 + 1) * TL],
                    in_=x_s[:, fc * TL:(fc + 1) * TL], func=AF.Square), inc="act")
                bld.wait("tensor", "act", t_sq)
                t_mm[fc] = bld.op("tensor", lambda e, fc=fc: e.matmul(
                    out=p_st2[:], lhsT=ones_s[:, 0:1],
                    rhs=sq_s[:, (fc % 2) * TL:(fc % 2 + 1) * TL],
                    start=(fc == 0), stop=(fc == 15)), inc="pe")
            t_stats = t_mm[15]
            bld.wait("vector", "pe", t_stats)
            bld.op("vector", lambda e: e.tensor_scalar_mul(
                out=row_m[:, 0:TL], in0=p_st[:], scalar1=1.0 / E))
            bld.op("vector", lambda e: e.tensor_tensor(
                out=row_b[:, 0:TL], in0=row_m[:, 0:TL], in1=row_m[:, 0:TL], op=ALU.mult))
            bld.op("vector", lambda e: e.scalar_tensor_tensor(
                out=row_c[:, 0:TL], in0=p_st2[:], scalar=1.0 / E, in1=row_b[:, 0:TL],
                op0=ALU.mult, op1=ALU.subtract))
            t_fin = bld.op("vector", lambda e: e.tensor_scalar_add(
                out=row_b[:, 0:TL], in0=row_c[:, 0:TL], scalar1=1e-5), inc="dve")
            bld.wait("scalar", "dve", t_fin)
            bld.op("scalar", lambda e: e.activation(
                out=row_r[:, 0:TL], in_=row_b[:, 0:TL], func=AF.Ln))
            t_r = bld.op("scalar", lambda e: e.activation(
                out=row_c[:, 0:TL], in_=row_r[:, 0:TL], func=AF.Exp, scale=-0.5),
                inc="act")
            bld.wait("tensor", "dve", t_fin)
            bld.op("tensor", lambda e: e.matmul(
                out=p_mb[:], lhsT=ones_s[0:1, :], rhs=row_m[0:1, 0:TL],
                start=True, stop=True))
            bld.wait("tensor", "act", t_r)
            t_bc = bld.op("tensor", lambda e: e.matmul(
                out=p_rb[:], lhsT=ones_s[0:1, :], rhs=row_c[0:1, 0:TL],
                start=True, stop=True), inc="pe")
            bld.wait("vector", "pe", t_bc)
            t_xn = None
            for fc in range(16):
                bld.op("vector", lambda e, fc=fc: e.tensor_tensor(
                    out=s_sub[:, 0:TL], in0=x_s[:, fc * TL:(fc + 1) * TL], in1=p_mb[:],
                    op=ALU.subtract))
                t_xn = bld.op("vector", lambda e, fc=fc: e.tensor_tensor(
                    out=xn_dst16(fc), in0=s_sub[:, 0:TL], in1=p_rb[:], op=ALU.mult),
                    inc=("dve" if fc == 15 else None))
            return t_xn

        with contextlib.ExitStack() as PS:
            wstg = PS.enter_context(nc.sbuf_tensor("wstg", [128, 1024], F32))
            cos_p = [wstg[:, 0:256], wstg[:, 256:512]]
            sin_p = [wstg[:, 512:768], wstg[:, 768:1024]]
            wk_b = PS.enter_context(nc.sbuf_tensor("wk_b", [128, H * 2048], BF16))
            wv_b = PS.enter_context(nc.sbuf_tensor("wv_b", [128, 4 * 8192], BF16))
            xp = [PS.enter_context(nc.sbuf_tensor(f"xp{i}", [128, 16 * 256], F32))
                  for i in range(2)]
            xnp = [PS.enter_context(nc.sbuf_tensor(f"xnp{i}", [128, 16 * 256], BF16))
                   for i in range(2)]
            kp_b = [PS.enter_context(nc.sbuf_tensor(f"kp_b{i}", [128, 256], BF16))
                    for i in range(2)]
            ktm = PS.enter_context(nc.sbuf_tensor("ktm", [128, 256], BF16))
            vp_b = PS.enter_context(nc.sbuf_tensor("vp_b", [128, 2 * 2048], BF16))
            sq_s = PS.enter_context(nc.sbuf_tensor("sq_p", [128, 2 * 256], F32))
            s1 = PS.enter_context(nc.sbuf_tensor("s1p", [128, 256], F32))
            s2 = PS.enter_context(nc.sbuf_tensor("s2p", [128, 256], F32))
            s3 = PS.enter_context(nc.sbuf_tensor("s3p", [128, 256], F32))
            rows_p = tuple(PS.enter_context(nc.sbuf_tensor(f"rowp{i}", [1, 256], F32))
                           for i in range(2))
            rows_p = (rows_p[0], rows_p[1], sq_s[0:1, 256:512], sq_s[0:1, 0:256])
            p_st = PS.enter_context(nc.psum_tensor("p_st", [1, 256], F32))
            p_st2 = PS.enter_context(nc.psum_tensor("p_st2", [1, 256], F32))
            p_mb = PS.enter_context(nc.psum_tensor("p_mb", [128, 256], F32))
            p_rb = PS.enter_context(nc.psum_tensor("p_rb", [128, 256], F32))
            p_k = PS.enter_context(nc.psum_tensor("p_k", [128, 256], F32))
            p_v = PS.enter_context(nc.psum_tensor("p_v", [128, 512], F32))
            p_tr = PS.enter_context(nc.psum_tensor("p_tr", [128, 256], BF16))
            p_su = PS.enter_context(nc.psum_tensor("p_su", [128, 129], F32))

            for which, dstb in (("k", wk_b), ("v", wv_b)):
                for i in range(32):
                    if which == "k":
                        piece = wk_d[i // 2][:, (i % 2) * 1024:(i % 2 + 1) * 1024]
                    else:
                        piece = wv_d[i // 8][:, (i % 8) * 1024:(i % 8 + 1) * 1024]
                    t_l = bld.dma("sync", wstg[:], piece, "dW")
                    eng, sem = (("vector", "dve") if i % 2 == 0 else ("scalar", "act"))
                    bld.wait(eng, "dW", t_l)
                    t = bld.op(eng, lambda e, i=i, dstb=dstb: _cp(e,
                        dstb[:, i * 1024:(i + 1) * 1024], wstg[:]), inc=sem)
                    bld.wait("sync", sem, t)
            bld.barrier()

            TL = 256
            t_xn_prev = 0
            t_xn_par = {0: 0, 1: 0}
            t_kv_done = {0: 0, 1: 0}
            t_rope_par = {0: 0, 1: 0}
            t_vevac = 0
            t_vmm = 0
            t_ktm_evac = 0
            t_kread_prev = 0
            t_sadd_prev = 0
            for tp in range(NTP):
                par = tp % 2
                xpb, xnpb = xp[par], xnp[par]
                cosb, sinb = cos_p[par], sin_p[par]
                bld.wait("sync", "dve", t_xn_par[par])
                bld.wait("sync", "dve", t_rope_par[par])
                bld.dma("sync", cosb[:], cos_pre_d[:, tp * 256:(tp + 1) * 256], "dW")
                bld.dma("sync", sinb[:], sins_pre_d[:, tp * 256:(tp + 1) * 256], "dW")
                t_x = bld.dma("sync", xpb[:], x_pre[tp], "dW")
                bld.wait("vector", "pe", t_kv_done[par])
                t_xn = ln_tile(xpb, lambda fc: xnpb[:, fc * TL:(fc + 1) * TL], TL,
                               sq_s, s1, p_st, p_st2, p_mb, p_rb, t_x, t_xn_prev,
                               rows=rows_p)
                t_xn_prev = t_xn
                t_xn_par[par] = t_xn
                bld.wait("tensor", "dve", t_xn)
                for ts in range(2):
                    for hg in range(4):
                        bld.wait("tensor", "act", t_vevac)
                        for fc in range(16):
                            t_vmm = bld.op("tensor", lambda e, ts=ts, hg=hg, fc=fc, xnpb=xnpb: e.matmul(
                                out=p_v[:],
                                lhsT=xnpb[:, fc * TL + ts * 128:fc * TL + ts * 128 + 128],
                                rhs=wv_b[:, hg * 8192 + fc * 512:hg * 8192 + (fc + 1) * 512],
                                start=(fc == 0), stop=(fc == 15)),
                                inc=("pe" if fc == 15 else None))
                        bld.wait("scalar", "pe", t_vmm)
                        t_vevac = bld.op("scalar", lambda e, ts=ts, hg=hg: e.copy(
                            out=vp_b[:, ts * 2048 + hg * 512:ts * 2048 + (hg + 1) * 512],
                            in_=p_v[:]), inc="act")
                t_vall = t_vevac
                t_kdone_by_h = {}
                t_tr_prev = 0
                for h in range(H + 1):
                    if h < H:
                        bld.wait("tensor", "dve", t_kread_prev)
                        for fc in range(16):
                            t_kmm = bld.op("tensor", lambda e, h=h, fc=fc, xnpb=xnpb: e.matmul(
                                out=p_k[:],
                                lhsT=wk_b[:, h * 2048 + fc * 128:h * 2048 + fc * 128 + 128],
                                rhs=xnpb[:, fc * TL:(fc + 1) * TL],
                                start=(fc == 0), stop=(fc == 15)),
                                inc=("pe" if fc == 15 else None))
                        bld.wait("vector", "pe", t_kmm)
                        bld.wait("vector", "dW", t_x)
                        bld.wait("vector", "pe", t_tr_prev)
                        t_kread_prev, t_kdone = _rope_elu(
                            bld, p_k, cosb[:, 0:TL],
                            sinb[0:64, 0:TL], sinb[64:128, 0:TL],
                            s1, s2, s3, kp_b[h % 2][:, 0:TL], TL)
                        t_rope_par[par] = t_kread_prev
                        t_kdone_by_h[h] = t_kdone
                    if h > 0:
                        hp = h - 1
                        bld.wait("tensor", "dve", t_kdone_by_h.pop(hp))
                        bld.wait("tensor", "act", t_ktm_evac)
                        t_trmm = None
                        for j in range(2):
                            t_trmm = bld.op("tensor", lambda e, j=j, hp=hp: e.transpose(
                                out=p_tr[:, j * 128:(j + 1) * 128],
                                in_=kp_b[hp % 2][:, j * 128:(j + 1) * 128],
                                identity=ident_b[:]),
                                inc=("pe" if j == 1 else None))
                        t_tr_prev = t_trmm
                        bld.wait("scalar", "pe", t_trmm)
                        t_ktm_evac = bld.op("scalar", lambda e: e.copy(
                            out=ktm[:], in_=p_tr[:]), inc="act")
                        bld.wait("tensor", "act", t_ktm_evac)
                        bld.wait("tensor", "act", t_vall)
                        bld.wait("tensor", "dve", t_sadd_prev)
                        t_su = None
                        for ts in range(2):
                            voff = ts * 2048 + (hp // 4) * 512 + (hp % 4) * 128
                            bld.op("tensor", lambda e, ts=ts, voff=voff: e.matmul(
                                out=p_su[:, 0:128], lhsT=ktm[:, ts * 128:(ts + 1) * 128],
                                rhs=vp_b[:, voff:voff + 128],
                                start=(ts == 0), stop=False))
                            t_su = bld.op("tensor", lambda e, ts=ts: e.matmul(
                                out=p_su[:, 128:129], lhsT=ktm[:, ts * 128:(ts + 1) * 128],
                                rhs=aug_b[:], start=False, stop=(ts == 1)),
                                inc=("pe" if ts == 1 else None))
                        bld.wait("vector", "pe", t_su)
                        t_sadd_prev = bld.op("vector", lambda e, hp=hp: e.tensor_tensor(
                            out=S_all[:, hp * 129:(hp + 1) * 129],
                            in0=S_all[:, hp * 129:(hp + 1) * 129],
                            in1=p_su[:], op=ALU.add), inc="dve")
                bld.flush()
            bld.barrier()

        if debug:
            bld.dma("scalar", dbg["S"], S_all[:], "dY")
            bld.barrier()
        if phases <= 2:
            raise _Cut

        def _main_scope():
          with contextlib.ExitStack() as MS:
            xn = MS.enter_context(nc.sbuf_tensor("xn", [128, 16 * TOK], BF16))
            cos_o = MS.enter_context(nc.sbuf_tensor("cos_o", [128, TOK], F32))
            sin_o = MS.enter_context(nc.sbuf_tensor("sin_o", [128, TOK], F32))
            bld.dma("sync", cos_o[:], cos_own_d, "dW")
            bld.dma("sync", sin_o[:], sins_own_d, "dW")

            with contextlib.ExitStack() as P0:
                x_tt = P0.enter_context(nc.sbuf_tensor("x_tt", [128, 16 * 512], F32))
                sq0 = P0.enter_context(nc.sbuf_tensor("sq0", [128, 2 * 512], F32))
                s10 = P0.enter_context(nc.sbuf_tensor("s10", [128, 512], F32))
                rows_0 = tuple(P0.enter_context(nc.sbuf_tensor(f"row0{i}", [1, 512], F32))
                               for i in range(4))
                p_st = P0.enter_context(nc.psum_tensor("p_st0", [1, 512], F32))
                p_st2 = P0.enter_context(nc.psum_tensor("p_st20", [1, 512], F32))
                p_mb = P0.enter_context(nc.psum_tensor("p_mb0", [128, 512], F32))
                p_rb = P0.enter_context(nc.psum_tensor("p_rb0", [128, 512], F32))
                t_prev = 0
                for tt in range(NT):
                    bld.wait("sync", "dve", t_prev)
                    t_x = bld.dma("sync", x_tt[:], x_own[tt], "dW")
                    t_prev = ln_tile(
                        x_tt,
                        lambda fc, tt=tt: xn[:, fc * TOK + tt * 512:fc * TOK + (tt + 1) * 512],
                        512, sq0, s10, p_st, p_st2, p_mb, p_rb, t_x, t_prev,
                        rows=rows_0)
                    bld.flush()
                bld.barrier()

            if debug:
                bld.dma("scalar", dbg["xn"], xn[:], "dY")
                bld.barrier()
            if phases <= 3:
                return

            with contextlib.ExitStack() as AS:
                v_all = AS.enter_context(nc.sbuf_tensor("v_all", [128, 8 * 16 * 128], BF16))
                wqk_st = AS.enter_context(nc.sbuf_tensor("wqk_st", [128, 1024], F32))
                wq_b = [AS.enter_context(nc.sbuf_tensor(f"wq_b{i}", [128, 2048], BF16))
                        for i in range(2)]
                wk_b2 = [AS.enter_context(nc.sbuf_tensor(f"wk_b2{i}", [128, 2048], BF16))
                         for i in range(2)]
                q_fm = [AS.enter_context(nc.sbuf_tensor(f"q_fm{i}", [128, TOK], BF16))
                        for i in range(2)]
                k_fm = [AS.enter_context(nc.sbuf_tensor(f"k_fm{i}", [128, TOK], BF16))
                        for i in range(2)]
                k_tm = [AS.enter_context(nc.sbuf_tensor(f"k_tm{i}", [128, TOK], BF16))
                        for i in range(2)]
                s1 = AS.enter_context(nc.sbuf_tensor("s1a", [128, 512], F32))
                s2 = AS.enter_context(nc.sbuf_tensor("s2a", [128, 512], F32))
                s3 = AS.enter_context(nc.sbuf_tensor("s3a", [128, 512], F32))
                attT_s = AS.enter_context(nc.sbuf_tensor("attT_s", [128, 384], BF16))
                den_s = AS.enter_context(nc.sbuf_tensor("den_s", [1, 256], F32))
                tri_s = AS.enter_context(nc.sbuf_tensor("tri_s", [128, 128], F32))
                S_bf = AS.enter_context(nc.sbuf_tensor("S_bf", [128, 129], BF16))
                rec_s = AS.enter_context(nc.sbuf_tensor("rec_s", [1, 256], F32))
                rec2_s = AS.enter_context(nc.sbuf_tensor("rec2_s", [1, 256], F32))
                rb_s = AS.enter_context(nc.sbuf_tensor("rb_s", [128, 256], F32))
                ao = [AS.enter_context(nc.sbuf_tensor(f"ao{i}", [128, TOK], BF16))
                      for i in range(2)]
                p_q = AS.enter_context(nc.psum_tensor("p_q", [128, 512], F32))
                p_kp = AS.enter_context(nc.psum_tensor("p_kp", [128, 512], F32))
                p_a = AS.enter_context(nc.psum_tensor("p_a", [128, 384], F32))
                p_n = AS.enter_context(nc.psum_tensor("p_n", [128, 256], F32))
                p_d = AS.enter_context(nc.psum_tensor("p_d", [1, 256], F32))
                p_r = AS.enter_context(nc.psum_tensor("p_r", [128, 256], F32))
                p_tr = AS.enter_context(nc.psum_tensor("p_tr2", [128, 256], BF16))
                p_su = AS.enter_context(nc.psum_tensor("p_su2", [128, 129], F32))

                bld.dma("sync", tri_s[:], tri_d, "dW")
                st = dict(vevac=0, vmm_last=0, qpsum_free=0, kpsum_free=0, kdone=0,
                          stage_dve=0, stage_act=0,
                          wq_used={0: 0, 1: 0}, wk_used={0: 0, 1: 0},
                          ktm_evac=0, q_bank_guard=0, cq={0: 0, 1: 0}, ck={0: 0, 1: 0},
                          lq={0: 0, 1: 0}, lk={0: 0, 1: 0})
                t_spill = {0: 0, 1: 0}

                def load_wqk(h):
                    par = h % 2
                    for pc in range(2):
                        bld.wait("sync", "dve", st["stage_dve"])
                        bld.wait("sync", "act", st["stage_act"])
                        st["lq"][par] = bld.dma("sync", wqk_st[:],
                                                wq_d[h][:, pc * 1024:(pc + 1) * 1024], "dW")
                        bld.wait("vector", "dW", st["lq"][par])
                        bld.wait("vector", "pe", st["wq_used"][par])
                        st["cq"][par] = bld.op("vector", lambda e, par=par, pc=pc: e.tensor_copy(
                            out=wq_b[par][:, pc * 1024:(pc + 1) * 1024], in_=wqk_st[:]),
                            inc="dve")
                        st["stage_dve"] = st["cq"][par]
                    for pc in range(2):
                        bld.wait("sync", "dve", st["stage_dve"])
                        bld.wait("sync", "act", st["stage_act"])
                        st["lk"][par] = bld.dma("sync", wqk_st[:],
                                                wk_d[h][:, pc * 1024:(pc + 1) * 1024], "dW")
                        bld.wait("scalar", "dW", st["lk"][par])
                        bld.wait("scalar", "pe", st["wk_used"][par])
                        st["ck"][par] = bld.op("scalar", lambda e, par=par, pc=pc: _cp(
                            e, wk_b2[par][:, pc * 1024:(pc + 1) * 1024], wqk_st[:]),
                            inc="act")
                        st["stage_act"] = st["ck"][par]

                def emit_trans_tt(h, tt):
                    par = h % 2
                    for ts4 in range(2):
                        bld.wait("tensor", "act", st["ktm_evac"])
                        t_trmm = None
                        for j in range(2):
                            sub = tt * 4 + ts4 * 2 + j
                            t_trmm = bld.op("tensor", lambda e, sub=sub, j=j, par=par: e.transpose(
                                out=p_tr[:, j * 128:(j + 1) * 128],
                                in_=k_fm[par][:, sub * 128:(sub + 1) * 128],
                                identity=ident_b[:]),
                                inc=("pe" if j == 1 else None))
                        bld.wait("scalar", "pe", t_trmm)
                        base = (tt * 4 + ts4 * 2) * 128
                        st["ktm_evac"] = bld.op("scalar", lambda e, base=base, par=par: e.copy(
                            out=k_tm[par][:, base:base + 256], in_=p_tr[:]), inc="act")

                def emit_proj_tt(h, tt, transposes=True, do_q=True, do_k=True):
                    par = h % 2
                    if do_q:
                        bld.wait("tensor", "dve", st["cq"][par])
                        bld.wait("tensor", "act", st["q_bank_guard"])
                        st["q_bank_guard"] = 0
                        bld.wait("tensor", "dve", st["qpsum_free"])
                        for fc in range(16):
                            t_qmm = bld.op("tensor", lambda e, tt=tt, fc=fc, par=par: e.matmul(
                                out=p_q[:], lhsT=wq_b[par][:, fc * 128:(fc + 1) * 128],
                                rhs=xn[:, fc * TOK + tt * 512:fc * TOK + (tt + 1) * 512],
                                start=(fc == 0), stop=(fc == 15)),
                                inc=("pe" if fc == 15 else None))
                        st["wq_used"][par] = t_qmm
                        bld.wait("vector", "pe", t_qmm)
                        st["qpsum_free"], _ = _rope_elu(
                            bld, p_q, cos_o[:, tt * 512:(tt + 1) * 512],
                            sin_o[0:64, tt * 512:(tt + 1) * 512],
                            sin_o[64:128, tt * 512:(tt + 1) * 512], s1, s2, s3,
                            q_fm[par][:, tt * 512:(tt + 1) * 512], 512)
                    if do_k:
                        bld.wait("tensor", "act", st["ck"][par])
                        bld.wait("tensor", "dve", st["kpsum_free"])
                        for fc in range(16):
                            t_kmm = bld.op("tensor", lambda e, tt=tt, fc=fc, par=par: e.matmul(
                                out=p_kp[:], lhsT=wk_b2[par][:, fc * 128:(fc + 1) * 128],
                                rhs=xn[:, fc * TOK + tt * 512:fc * TOK + (tt + 1) * 512],
                                start=(fc == 0), stop=(fc == 15)),
                                inc=("pe" if fc == 15 else None))
                        st["wk_used"][par] = t_kmm
                        bld.wait("vector", "pe", t_kmm)
                        st["kpsum_free"], t_kdone = _rope_elu(
                            bld, p_kp, cos_o[:, tt * 512:(tt + 1) * 512],
                            sin_o[0:64, tt * 512:(tt + 1) * 512],
                            sin_o[64:128, tt * 512:(tt + 1) * 512], s1, s2, s3,
                            k_fm[par][:, tt * 512:(tt + 1) * 512], 512)
                        st["kdone"] = t_kdone
                        if transposes:
                            bld.wait("tensor", "dve", t_kdone)
                            emit_trans_tt(h, tt)

                for grp in range(A_GRP):
                    with nc.sbuf_tensor(f"wv_bg{grp}", [128, 8192], BF16) as wv_b, \
                         nc.sbuf_tensor(f"wv_stg{grp}", [128, 1024], F32) as wv_st:
                        for hgl in range(2):
                            hg = grp * 2 + hgl
                            bld.wait("sync", "pe", st["vmm_last"])
                            t_cast = 0
                            for pc in range(8):
                                bld.wait("sync", "dve", t_cast)
                                t_l = bld.dma("sync", wv_st[:],
                                              wv_d[hg][:, pc * 1024:(pc + 1) * 1024], "dW")
                                bld.wait("vector", "dW", t_l)
                                t_cast = bld.op("vector", lambda e, pc=pc: e.tensor_copy(
                                    out=wv_b[:, pc * 1024:(pc + 1) * 1024], in_=wv_st[:]),
                                    inc="dve")
                            for ts in range(16):
                                bld.wait("tensor", "dve", t_cast)
                                bld.wait("tensor", "act", st["vevac"])
                                bld.wait("tensor", "dve", st["qpsum_free"])
                                for fc in range(16):
                                    st["vmm_last"] = bld.op("tensor", lambda e, ts=ts, fc=fc: e.matmul(
                                        out=p_q[:],
                                        lhsT=xn[:, fc * TOK + ts * 128:fc * TOK + ts * 128 + 128],
                                        rhs=wv_b[:, fc * 512:(fc + 1) * 512],
                                        start=(fc == 0), stop=(fc == 15)),
                                        inc=("pe" if fc == 15 else None))
                                bld.wait("scalar", "pe", st["vmm_last"])
                                st["vevac"] = bld.op("scalar", lambda e, hgl=hgl, ts=ts: e.copy(
                                    out=v_all[:, (hgl * 16 + ts) * 512:(hgl * 16 + ts + 1) * 512],
                                    in_=p_q[:]), inc="act")
                            st["qpsum_free"] = 0
                        bld.flush()
                    st["q_bank_guard"] = st["vevac"]
                    load_wqk(grp * 8)
                    for tt in range(NT):
                        emit_proj_tt(grp * 8, tt)
                    bld.flush()
                    for hl in range(A_HEADS):
                        h = grp * 8 + hl
                        par = h % 2
                        aob = ao[par]
                        qf, kf, ktm_ = q_fm[par], k_fm[par], k_tm[par]
                        if hl < 7:
                            load_wqk(h + 1)
                            if NO_ILV:
                                for tt in range(NT):
                                    emit_proj_tt(h + 1, tt)
                        if not NO_ILV and hl > 0:
                            if ILV_K == 0:
                                for _tt in range(NT):
                                    emit_proj_tt(h, _tt, transposes=False,
                                                 do_q=False, do_k=True)
                            elif ILV_K == 2:
                                for _tt in range(NT):
                                    emit_proj_tt(h, _tt, transposes=False,
                                                 do_q=True, do_k=False)
                            bld.wait("tensor", "dve", st["kdone"])
                            for _tt in range(NT):
                                emit_trans_tt(h, _tt)
                        bld.wait("scalar", "dve", bld.sem["dve"].n)
                        t_sbf = bld.op("scalar", lambda e, h=h: e.copy(
                            out=S_bf[:], in_=S_all[:, h * 129:(h + 1) * 129]), inc="act")
                        bld.wait("vector", "dY", t_spill[par])
                        t_evD = t_evA = t_denf = t_rec = t_rbev = t_out = 0
                        t_numm = t_summ = t_sadd = t_denm = t_rbmm = 0
                        for c in range(A_CHUNKS):
                            qs256 = qf[:, c * 256:(c + 1) * 256]
                            ks_m0 = kf[:, c * 256:c * 256 + 128]
                            ks_m1 = kf[:, c * 256 + 128:(c + 1) * 256]
                            qs_l1 = qf[:, c * 256 + 128:(c + 1) * 256]
                            bld.wait("tensor", "dve", t_evD)
                            bld.wait("tensor", "act", t_evA)
                            bld.op("tensor", lambda e, a=ks_m0, b=qs256: e.matmul(
                                out=p_a[:, 0:256], lhsT=a, rhs=b, start=True, stop=False))
                            t_att = bld.op("tensor", lambda e, a=ks_m1, b=qs_l1: e.matmul(
                                out=p_a[:, 256:384], lhsT=a, rhs=b, start=False, stop=True),
                                inc="pe")
                            bld.wait("vector", "pe", t_att)
                            bld.op("vector", lambda e: e.tensor_tensor(
                                out=attT_s[:, 0:128], in0=p_a[:, 0:128], in1=tri_s[:],
                                op=ALU.mult))
                            t_evD = bld.op("vector", lambda e: e.tensor_tensor(
                                out=attT_s[:, 256:384], in0=p_a[:, 256:384], in1=tri_s[:],
                                op=ALU.mult), inc="dve")
                            bld.wait("scalar", "pe", t_att)
                            t_evA = bld.op("scalar", lambda e: e.copy(
                                out=attT_s[:, 128:256], in_=p_a[:, 128:256]), inc="act")
                            bld.wait("tensor", "dve", t_evD)
                            bld.wait("tensor", "act", t_evA)
                            bld.wait("tensor", "act", t_sbf)
                            bld.wait("tensor", "act", st["vevac"])
                            bld.wait("tensor", "dve", t_out)

                            def _voff(sub, hl=hl):
                                return ((hl // 4) * 16 + sub) * 512 + (hl % 4) * 128
                            v0o, v1o = _voff(2 * c), _voff(2 * c + 1)
                            bld.op("tensor", lambda e, v0o=v0o: e.matmul(
                                out=p_n[:, 0:256], lhsT=v_all[:, v0o:v0o + 128],
                                rhs=attT_s[:, 0:256], start=True, stop=False))
                            bld.op("tensor", lambda e, v1o=v1o: e.matmul(
                                out=p_n[:, 128:256], lhsT=v_all[:, v1o:v1o + 128],
                                rhs=attT_s[:, 256:384], start=False, stop=False))
                            t_numm = bld.op("tensor", lambda e, b=qs256: e.matmul(
                                out=p_n[:, 0:256], lhsT=S_bf[:, 0:128], rhs=b,
                                start=False, stop=True), inc="pe")
                            bld.wait("tensor", "dve", t_denf)
                            bld.op("tensor", lambda e: e.matmul(
                                out=p_d[:, 0:256], lhsT=onescol_b[:], rhs=attT_s[:, 0:256],
                                start=True, stop=False))
                            bld.op("tensor", lambda e: e.matmul(
                                out=p_d[:, 128:256], lhsT=onescol_b[:], rhs=attT_s[:, 256:384],
                                start=False, stop=False))
                            t_denm = bld.op("tensor", lambda e, b=qs256: e.matmul(
                                out=p_d[:, 0:256], lhsT=S_bf[:, 128:129], rhs=b,
                                start=False, stop=True), inc="pe")
                            bld.wait("tensor", "act", st["ktm_evac"])
                            bld.wait("tensor", "dve", t_sadd)
                            for j in range(2):
                                sub = 2 * c + j
                                kslT = ktm_[:, sub * 128:(sub + 1) * 128]
                                vo = _voff(sub)
                                vsl = v_all[:, vo:vo + 128]
                                bld.op("tensor", lambda e, a=kslT, b=vsl, j=j: e.matmul(
                                    out=p_su[:, 0:128], lhsT=a, rhs=b,
                                    start=(j == 0), stop=False))
                                t_summ = bld.op("tensor", lambda e, a=kslT, j=j: e.matmul(
                                    out=p_su[:, 128:129], lhsT=a, rhs=onescol_b[:],
                                    start=False, stop=(j == 1)),
                                    inc=("pe" if j == 1 else None))
                            bld.wait("vector", "pe", t_denm)
                            bld.wait("vector", "act", t_rec)
                            t_denf = bld.op("vector", lambda e: e.tensor_scalar_max(
                                out=den_s[:], in0=p_d[:], scalar1=1e-6), inc="dve")
                            bld.wait("scalar", "dve", t_denf)
                            bld.wait("scalar", "pe", t_rbmm)
                            bld.op("scalar", lambda e: e.activation(
                                out=rec_s[:], in_=den_s[:], func=AF.Ln))
                            t_rec = bld.op("scalar", lambda e: e.activation(
                                out=rec2_s[:], in_=rec_s[:], func=AF.Exp, scale=-1.0),
                                inc="act")
                            bld.wait("vector", "pe", t_summ)
                            bld.wait("vector", "act", t_sbf)
                            t_sadd = bld.op("vector", lambda e, h=h: e.tensor_tensor(
                                out=S_all[:, h * 129:(h + 1) * 129],
                                in0=S_all[:, h * 129:(h + 1) * 129],
                                in1=p_su[:], op=ALU.add), inc="dve")
                            bld.wait("scalar", "dve", t_sadd)
                            t_sbf = bld.op("scalar", lambda e, h=h: e.copy(
                                out=S_bf[:], in_=S_all[:, h * 129:(h + 1) * 129]), inc="act")
                            bld.wait("tensor", "act", t_rec)
                            bld.wait("tensor", "act", t_rbev)
                            t_rbmm = bld.op("tensor", lambda e: e.matmul(
                                out=p_r[:], lhsT=ones_s[0:1, :], rhs=rec2_s[0:1, :],
                                start=True, stop=True), inc="pe")
                            bld.wait("scalar", "pe", t_rbmm)
                            t_rbev = bld.op("scalar", lambda e: e.copy(
                                out=rb_s[:], in_=p_r[:]), inc="act")
                            bld.wait("vector", "pe", t_numm)
                            bld.wait("vector", "act", t_rbev)
                            t_out = bld.op("vector", lambda e, c=c, aob=aob: e.tensor_tensor(
                                out=aob[:, c * 256:(c + 1) * 256], in0=p_n[:],
                                in1=rb_s[:], op=ALU.mult), inc="dve")
                            if hl < 7 and not NO_ILV and c % 2 == 1:
                                emit_proj_tt(h + 1, c // 2, transposes=False,
                                             do_q=(ILV_K != 2), do_k=(ILV_K in (1, 2)))
                        bld.wait("scalar", "dve", t_out)
                        t_sp = 0
                        for tt in range(NT):
                            t_sp = bld.dma("scalar", sp_attn[tt][h],
                                           aob[:, tt * 512:(tt + 1) * 512], "dY")
                        t_spill[par] = t_sp
                        if debug and h == 0:
                            bld.dma("scalar", dbg["q0"], qf[:], "dY")
                            bld.dma("scalar", dbg["k0"], kf[:], "dY")
                            bld.dma("scalar", dbg["att0"], aob[:], "dY")
                            t_dbg = bld.dma("scalar", dbg["v0"], v_all[:, 0:2048], "dY")
                            for eng in ("tensor", "vector", "scalar", "sync", "gpsimd"):
                                bld.wait(eng, "dY", t_dbg)
                        bld.flush()
                bld.barrier()
            if debug:
                for tt in range(NT):
                    for hh in range(H):
                        bld.dma("scalar",
                                dbg["attnall"][:, hh * TOK + tt * 512:hh * TOK + (tt + 1) * 512],
                                sp_attn[tt][hh], "dY")
                bld.barrier()

            with contextlib.ExitStack() as FS:
                wgu_st = [FS.enter_context(nc.sbuf_tensor(f"wgu_st{i}", [128, 4096], F32))
                          for i in range(2)]
                wgu_b = [FS.enter_context(nc.sbuf_tensor(f"wgu_b{i}", [128, 4096], BF16))
                         for i in range(2)]
                gated = [FS.enter_context(nc.sbuf_tensor(f"gated{i}", [128, TOK], BF16))
                         for i in range(2)]
                sg_s = FS.enter_context(nc.sbuf_tensor("sg_s", [128, 2 * 512], F32))
                wpp_st = [FS.enter_context(nc.sbuf_tensor(f"wpp_st{i}", [128, 2560], F32))
                          for i in range(2)]
                wpp_bf = [FS.enter_context(nc.sbuf_tensor(f"wpp_bf{i}", [128, 2560], BF16))
                          for i in range(2)]
                p_g = [FS.enter_context(nc.psum_tensor(f"p_g{i}", [128, 512], F32))
                       for i in range(2)]
                p_u = [FS.enter_context(nc.psum_tensor(f"p_u{i}", [128, 512], F32))
                       for i in range(2)]
                t_cast_par = {0: 0, 1: 0}
                t_lastmm_par = {0: 0, 1: 0}
                t_gev = {}
                t_spill = {0: 0, 1: 0}
                def emit_wpp_piece(ocp, pc):
                    par = pc % 2
                    eng, semn = (("vector", "dve") if par == 0 else ("scalar", "act"))
                    bld.wait("sync", semn, bld.t_wppc.get(par, 0))
                    t_l = bld.dma("sync", wpp_st[par][:],
                                  wdo_d[ocp][:, pc * 2560:(pc + 1) * 2560], "dW")
                    bld.wait(eng, "dW", t_l)
                    bld.wait(eng, "dY", bld.t_wpp.get(par, 0))
                    bld.t_wppc[par] = bld.op(eng, lambda e, par=par: _cp(
                        e, wpp_bf[par][:], wpp_st[par][:]), inc=semn)
                    bld.wait("scalar", semn, bld.t_wppc[par])
                    bld.t_wpp[par] = bld.dma(
                        "scalar", wdo_bf_d[ocp][:, pc * 2560:(pc + 1) * 2560],
                        wpp_bf[par][:], "dY")

                def load_wgu(hc):
                    par = hc % 2
                    stb, gb = wgu_st[par], wgu_b[par]
                    eng, semn = (("vector", "dve") if par == 0 else ("scalar", "act"))
                    bld.wait("sync", semn, t_cast_par[par])
                    t_l = bld.dma("sync", stb[:], wgu_d[hc], "dW")
                    bld.wait(eng, "dW", t_l)
                    bld.wait(eng, "pe", t_lastmm_par[par])
                    t_cast_par[par] = bld.op(eng, lambda e, stb=stb, gb=gb: _cp(
                        e, gb[:], stb[:]), inc=semn)

                load_wgu(0)
                for hc in range(64):
                    emit_wpp_piece(hc // 4, hc % 4)
                    if hc + 1 < 64:
                        load_wgu(hc + 1)
                    par = hc % 2
                    stb, gb, gtb = wgu_st[par], wgu_b[par], gated[par]
                    eng, semn = (("vector", "dve") if par == 0 else ("scalar", "act"))
                    t_umm = 0
                    for tt in range(NT):
                        pg, pu = p_g[tt % 2], p_u[tt % 2]
                        bld.wait("tensor", semn, t_cast_par[par])
                        bld.wait("tensor", "act", t_gev.get((tt % 2, 0), 0))
                        bld.wait("tensor", "dve", t_gev.get((tt % 2, 1), 0))
                        for fc in range(16):
                            bld.op("tensor", lambda e, fc=fc, tt=tt, pg=pg, gb=gb: e.matmul(
                                out=pg[:], lhsT=gb[:, fc * 128:(fc + 1) * 128],
                                rhs=xn[:, fc * TOK + tt * 512:fc * TOK + (tt + 1) * 512],
                                start=(fc == 0), stop=(fc == 15)))
                        for fc in range(16):
                            t_umm = bld.op("tensor", lambda e, fc=fc, tt=tt, pu=pu, gb=gb: e.matmul(
                                out=pu[:], lhsT=gb[:, 2048 + fc * 128:2048 + (fc + 1) * 128],
                                rhs=xn[:, fc * TOK + tt * 512:fc * TOK + (tt + 1) * 512],
                                start=(fc == 0), stop=(fc == 15)),
                                inc=("pe" if fc == 15 else None))
                        bld.wait("scalar", "pe", t_umm)
                        bld.wait("scalar", "dve", t_gev.get((tt % 2, 1), 0))
                        t_si = bld.op("scalar", lambda e, tt=tt, pg=pg: e.activation(
                            out=sg_s[:, (tt % 2) * 512:(tt % 2 + 1) * 512], in_=pg[:],
                            func=AF.Silu), inc="act")
                        t_gev[(tt % 2, 0)] = t_si
                        bld.wait("vector", "act", t_si)
                        bld.wait("vector", "dY", t_spill[par])
                        t_gv = bld.op("vector", lambda e, tt=tt, pu=pu, gtb=gtb: e.tensor_tensor(
                            out=gtb[:, tt * 512:(tt + 1) * 512],
                            in0=sg_s[:, (tt % 2) * 512:(tt % 2 + 1) * 512],
                            in1=pu[:], op=ALU.mult), inc="dve")
                        t_gev[(tt % 2, 1)] = t_gv
                    t_lastmm_par[par] = t_umm
                    bld.wait("scalar", "dve", t_gv)
                    for tt in range(NT):
                        t_sp = bld.dma("scalar", sp_gated[tt][hc],
                                       gtb[:, tt * 512:(tt + 1) * 512], "dY")
                    t_spill[par] = t_sp
                    if debug and hc == 0:
                        bld.dma("scalar", dbg["gated0"], gtb[:], "dY")
                    bld.flush()
                bld.barrier()
        _main_scope()
        if phases <= 5:
            raise _Cut

        SS.close()
        with contextlib.ExitStack() as ZS:
            g_t = [ZS.enter_context(nc.sbuf_tensor(f"g_t{i}", [128, 64 * 512], BF16))
                   for i in range(2)]
            a_t = [ZS.enter_context(nc.sbuf_tensor(f"a_t{i}", [128, 16 * 512], BF16))
                   for i in range(2)]
            wdo_b = [ZS.enter_context(nc.sbuf_tensor(f"wdo_b{i}", [128, 10240], BF16))
                     for i in range(2)]
            x_bl = [ZS.enter_context(nc.sbuf_tensor("x_bl0", [128, 512], F32))] * 2
            y_s = [ZS.enter_context(nc.sbuf_tensor(f"y_s{i}", [128, 512], F32))
                   for i in range(2)]
            p_y = [ZS.enter_context(nc.psum_tensor(f"p_y{i}", [128, 512], F32))
                   for i in range(2)]
            t_wld = {0: 0, 1: 0}
            t_wlast = {0: 0, 1: 0}
            t_glast = {0: 0, 1: 0}
            t_yev = {0: 0, 1: 0}
            t_ydma = {0: 0, 1: 0}
            t_xl = {0: 0, 1: 0}

            def load_tile(tt):
                par = tt % 2
                bld.wait("sync", "pe", t_glast[par])
                bld.dma("sync", g_t[par][:].rearrange("p (c t) -> p c t", c=64),
                        sp_gated[tt].rearrange("c p t -> p c t"), "dW")
                return bld.dma("sync", a_t[par][:].rearrange("p (c t) -> p c t", c=16),
                               sp_attn[tt].rearrange("c p t -> p c t"), "dW")

            t_tile = {0: load_tile(0), 1: load_tile(1)}
            if debug:
                bld.wait("scalar", "dW", t_tile[0])
                bld.dma("scalar", dbg["wdo0"], wdo_bf_d[0], "dY")
                bld.dma("scalar", dbg["gt0"], g_t[0][:], "dY")
                bld.dma("scalar", dbg["at0"], a_t[0][:], "dY")
            bld.barrier()
            for half in range(2):
                for oc in range(16):
                    opar = oc % 2
                    wb = wdo_b[opar]
                    bld.wait("sync", "pe", t_wlast[opar])
                    t_wld[opar] = bld.dma("sync", wb[:], wdo_bf_d[oc], "dW")
                    for tt2 in range(2):
                        tt = half * 2 + tt2
                        par = tt % 2
                        gb, ab = g_t[par], a_t[par]
                        bld.wait("scalar", "dve", max(t_yev[0], t_yev[1]))
                        t_xl[opar] = bld.dma("scalar", x_bl[opar][:],
                                             x_own[tt][:, oc * 512:(oc + 1) * 512], "dX")
                        py = p_y[opar]
                        bld.wait("tensor", "dW", t_tile[par])
                        bld.wait("tensor", "dW", t_wld[opar])
                        bld.wait("tensor", "dve", t_yev[opar])
                        for kc in range(64):
                            bld.op("tensor", lambda e, kc=kc, py=py, wb=wb, gb=gb: e.matmul(
                                out=py[:], lhsT=wb[:, kc * 128:(kc + 1) * 128],
                                rhs=gb[:, kc * 512:(kc + 1) * 512],
                                start=(kc == 0), stop=False))
                        for ec in range(16):
                            t_ymm = bld.op("tensor", lambda e, ec=ec, py=py, wb=wb, ab=ab: e.matmul(
                                out=py[:], lhsT=wb[:, (64 + ec) * 128:(64 + ec + 1) * 128],
                                rhs=ab[:, ec * 512:(ec + 1) * 512],
                                start=False, stop=(ec == 15)),
                                inc=("pe" if ec == 15 else None))
                        t_wlast[opar] = t_ymm
                        t_glast[par] = t_ymm
                        bld.wait("vector", "pe", t_ymm)
                        bld.wait("vector", "dX", t_xl[opar])
                        bld.wait("vector", "dY", t_ydma[opar])
                        t_yev[opar] = bld.op("vector", lambda e, opar=opar, py=py: e.tensor_tensor(
                            out=y_s[opar][:], in0=x_bl[opar][:], in1=py[:], op=ALU.add),
                            inc="dve")
                        bld.wait("scalar", "dve", t_yev[opar])
                        t_ydma[opar] = bld.dma("scalar",
                                               y_out[oc][:, tt * 512:(tt + 1) * 512],
                                               y_s[opar][:], "dY")
                    bld.flush()
                if half == 0:
                    t_tile[0] = load_tile(2)
                    t_tile[1] = load_tile(3)
            bld.wait("scalar", "dY", bld.sem["dY"].n)
            bld.barrier()

    except _Cut:
        pass
    es.close()
    return nc



_NC_CACHE = {}


def kernel(**inputs):
    in_maps = _stage(inputs)
    key = "dbg" if DEBUG else "rel"
    if key not in _NC_CACHE:
        key = (key, PHASES)
    if key not in _NC_CACHE:
        _NC_CACHE[key] = build_nc(debug=DEBUG)
    nc = _NC_CACHE[key]
    res = run_bass_kernel_spmd(nc, in_maps, core_ids=list(range(NCORES)))
    out = np.empty((B, S, E), np.float32)
    for c in range(NCORES):
        b, half = c // 2, c % 2
        yc = np.asarray(res.results[c]["y"])
        out[b, half * TOK:(half + 1) * TOK, :] = (
            yc.reshape(16, 128, TOK).transpose(2, 0, 1).reshape(TOK, E))
    if DEBUG:
        kernel._dbg = res.results
    return out
